# version 06
# baseline (speedup 1.0000x reference)
import numpy as np
import ml_dtypes
import concourse.bass as bass
import concourse.mybir as mybir
from concourse.bass import IndirectOffsetOnAxis
from concourse.bass_utils import run_bass_kernel_spmd

F32 = mybir.dt.float32
BF16 = mybir.dt.bfloat16
U32 = mybir.dt.uint32
AF = mybir.ActivationFunctionType
ALU = mybir.AluOpType
AX = mybir.AxisListType

N_CORES = 8
R = 1024
T = 8
L = 4096
RANK = 1024
OOB_OFF = 8388608.0
BOUND = 8388607
BF = ml_dtypes.bfloat16
DEBUG = False

_CACHE = {}


def build_consts():
    if _CACHE:
        return _CACHE
    c = {}
    a64 = np.arange(64)
    c32 = np.arange(32)
    mu64 = np.arange(64)

    th = 2 * np.pi * np.outer(a64, a64) / 64.0
    f1 = np.zeros((64, 128), np.float32)
    f1[:, 0:64] = np.cos(th)
    f1[:, 64:128] = -np.sin(th)
    c["f1const"] = f1.astype(BF)

    st1 = np.zeros((128, 64, 128), np.float32)
    st2 = np.zeros((128, 64, 128), np.float32)
    for d in range(64):
        k = 64 * c32 + d
        t2 = 2 * np.pi * np.outer(a64, k) / 4096.0
        cosv, sinv = np.cos(t2), np.sin(t2)
        for r2 in range(2):
            rs = slice(64 * r2, 64 * r2 + 64)
            st1[rs, d, 0 + 32 * r2: 32 + 32 * r2] = cosv
            st1[rs, d, 64 + 32 * r2: 96 + 32 * r2] = -sinv
            st2[rs, d, 0 + 32 * r2: 32 + 32 * r2] = sinv
            st2[rs, d, 64 + 32 * r2: 96 + 32 * r2] = cosv
    c["f2st1"] = st1.reshape(128, 64 * 128).astype(BF)
    c["f2st2"] = st2.reshape(128, 64 * 128).astype(BF)

    e1 = np.zeros((128, 4), np.float32)
    e2 = np.zeros((128, 4), np.float32)
    sgn = ((-1.0) ** a64).astype(np.float32)
    for r2 in range(2):
        e1[64 * r2: 64 * r2 + 64, 0 + r2] = sgn
        e2[64 * r2: 64 * r2 + 64, 2 + r2] = sgn
    c["e2048a"] = e1.astype(BF)
    c["e2048b"] = e2.astype(BF)
    f2048 = np.zeros((4, 2), np.float32)
    f2048[0, 0] = f2048[2, 0] = 1.0
    f2048[1, 1] = f2048[3, 1] = 1.0
    c["fold2048"] = f2048

    fold = np.zeros((128, 64), np.float32)
    for csx in range(2):
        for r2 in range(2):
            for ci in range(32):
                fold[csx * 64 + r2 * 32 + ci, r2 * 32 + ci] = 1.0
    c["fold"] = fold.astype(BF)

    thw = 2 * np.pi * np.outer(c32, mu64) / 64.0
    wv = np.zeros((32, 128), np.float32)
    wv[:, 0:64] = np.cos(thw)
    wv[:, 64:128] = np.sin(thw)
    winv = np.zeros((128, 128), np.float32)
    winv[0:32] = wv
    winv[64:96] = wv
    c["winv"] = winv.astype(BF)

    lam = np.arange(64)
    fcos = np.zeros((128, 64, 64), np.float32)
    fsin = np.zeros((128, 64, 64), np.float32)
    for m_ in range(64):
        lv = 64 * lam + m_
        tb = 2 * np.pi * np.outer(a64, lv) / 4096.0
        fc = (2.0 / 4096.0) * np.cos(tb)
        fs = (-2.0 / 4096.0) * np.sin(tb)
        fcos[0:64, m_] = fc
        fcos[64:128, m_] = fc
        fsin[0:64, m_] = fs
        fsin[64:128, m_] = fs
    c["fcos"] = fcos.reshape(128, 64 * 64).astype(BF)
    c["fsin"] = fsin.reshape(128, 64 * 64).astype(BF)

    c["signmu"] = np.tile(((-1.0) ** mu64).astype(np.float32), (128, 1))
    c["iotap"] = (np.arange(128, dtype=np.float32) * 8192.0).reshape(128, 1)
    c["ident"] = np.eye(128, dtype=np.float32)
    c["ones1"] = np.ones((1, 128), np.float32)
    c["epsc"] = np.full((128, 1), 1e-5, np.float32)
    c["onescol"] = np.ones((128, 1), np.float32).astype(BF)
    _CACHE.update(c)
    return c


class Sched:
    ENGINES = ("sync", "vector", "scalar", "tensor", "gpsimd")

    def __init__(self):
        self.script = {e: [] for e in self.ENGINES}
        self.counts = {}

    def new_sem(self, name):
        self.counts[name] = 0

    def wait(self, eng, sem, cnt=None):
        owner = self.SEM_OWNER.get(sem)
        if (owner is not None and owner != eng
                and (cnt is None or cnt >= self.counts[sem])
                and getattr(self, "flush_fns", None)):
            cnt = self.flush(owner)
        c = self.counts[sem] if cnt is None else cnt
        if c > 0:
            self.script[eng].append(("wait", sem, c))

    SELF_SEM = {"vector": "vS", "scalar": "sS"}
    SEM_OWNER = {"vS": "vector", "sS": "scalar"}

    def __init_flush__(self):
        self.flush_fns = {}
        self.last_was_flush = {"vector": False, "scalar": False}

    def op(self, eng, fn, inc=None, inc_n=1, _is_flush=False):
        own = self.SELF_SEM.get(eng)
        if own is not None and self.counts.get(own, 0) > 0:
            self.script[eng].append(("wait", own, self.counts[own]))
        self.script[eng].append(("op", fn, inc, inc_n))
        if eng in self.last_was_flush:
            self.last_was_flush[eng] = _is_flush
        if inc is not None:
            self.counts[inc] += inc_n
            return self.counts[inc]
        return None

    def flush(self, owner):
        if self.last_was_flush.get(owner):
            return self.counts[self.SELF_SEM[owner]]
        return self.op(owner, self.flush_fns[owner], self.SELF_SEM[owner], _is_flush=True)

    def cnt(self, sem):
        return self.counts[sem]

    def barrier(self):
        snap = dict(self.counts)
        for eng in self.ENGINES:
            for sem, c in snap.items():
                self.wait(eng, sem, c)

    def emit(self, eng_name, eng, sems):
        for item in self.script[eng_name]:
            if item[0] == "wait":
                eng.wait_ge(sems[item[1]], item[2])
            else:
                _, fn, inc, inc_n = item
                instr = fn(eng)
                if inc is not None:
                    instr.then_inc(sems[inc], inc_n)


def build_program():
    nc = bass.Bass("TRN2", target_bir_lowering=False)

    d_xr = nc.dram_tensor("xr", [R, L], F32, kind="ExternalInput").ap()
    d_xf3 = nc.dram_tensor("xf3", [64, R * 64], F32, kind="ExternalInput").ap()
    d_xt = nc.dram_tensor("xt", [L, R], F32, kind="ExternalInput").ap()
    d_v = nc.dram_tensor("vw", [L, RANK], F32, kind="ExternalInput").ap()
    d_ut = nc.dram_tensor("ut", [RANK, L], F32, kind="ExternalInput").ap()
    d_s = nc.dram_tensor("sw", [128, 8], F32, kind="ExternalInput").ap()
    d_out = nc.dram_tensor("out", [R, L], F32, kind="ExternalOutput").ap()
    if DEBUG:
        d_dbg_st = nc.dram_tensor("dbg_st", [128, 6 * T], F32, kind="ExternalOutput").ap()
        d_dbg_corr = nc.dram_tensor("dbg_corr", [128, L], F32, kind="ExternalOutput").ap()
        d_dbg_v8 = nc.dram_tensor("dbg_v8", [128, 8], F32, kind="ExternalOutput").ap()
        d_dbg_i8 = nc.dram_tensor("dbg_i8", [128, 8], U32, kind="ExternalOutput").ap()
        d_dbg_w8 = nc.dram_tensor("dbg_w8", [128, 8], F32, kind="ExternalOutput").ap()
        d_dbg_g = nc.dram_tensor("dbg_g", [128, L], BF16, kind="ExternalOutput").ap()
        d_dbg_s = nc.dram_tensor("dbg_s", [128, L], BF16, kind="ExternalOutput").ap()
        d_dbg_t1 = nc.dram_tensor("dbg_t1", [128, 8 * R], BF16, kind="ExternalOutput").ap()
        d_dbg_gsb = nc.dram_tensor("dbg_gsb", [128, 8192], BF16, kind="ExternalOutput").ap()
        d_dbg_psb = nc.dram_tensor("dbg_psb", [128, 4096], BF16, kind="ExternalOutput").ap()
        d_dbg_ysb = nc.dram_tensor("dbg_ysb", [128, 8192], BF16, kind="ExternalOutput").ap()
        d_dbg_vsb = nc.dram_tensor("dbg_vsb", [128, 1024], BF16, kind="ExternalOutput").ap()
        d_dbg_ib = nc.dram_tensor("dbg_ib", [128, 1024], F32, kind="ExternalOutput").ap()
        d_dbg_cs = nc.dram_tensor("dbg_cs", [128, 8], F32, kind="ExternalOutput").ap()
        d_dbg_ps = nc.dram_tensor("dbg_ps", [128, 512], F32, kind="ExternalOutput").ap()
    d_xdup = nc.dram_tensor("xdup", [R, 2 * L], BF16, kind="Internal").ap()
    d_gsp = nc.dram_tensor("gsp", [R, L], BF16, kind="Internal").ap()

    d_f1 = nc.dram_tensor("f1const", [64, 128], BF16, kind="ExternalInput").ap()
    d_f2st1 = nc.dram_tensor("f2st1", [128, 64 * 128], BF16, kind="ExternalInput").ap()
    d_f2st2 = nc.dram_tensor("f2st2", [128, 64 * 128], BF16, kind="ExternalInput").ap()
    d_e2048a = nc.dram_tensor("e2048a", [128, 4], BF16, kind="ExternalInput").ap()
    d_e2048b = nc.dram_tensor("e2048b", [128, 4], BF16, kind="ExternalInput").ap()
    d_fold2048 = nc.dram_tensor("fold2048", [4, 2], F32, kind="ExternalInput").ap()
    d_fold = nc.dram_tensor("fold", [128, 64], BF16, kind="ExternalInput").ap()
    d_winv = nc.dram_tensor("winv", [128, 128], BF16, kind="ExternalInput").ap()
    d_fcos = nc.dram_tensor("fcos", [128, 64 * 64], BF16, kind="ExternalInput").ap()
    d_fsin = nc.dram_tensor("fsin", [128, 64 * 64], BF16, kind="ExternalInput").ap()
    d_signmu = nc.dram_tensor("signmu", [128, 64], F32, kind="ExternalInput").ap()
    d_iotap = nc.dram_tensor("iotap", [128, 1], F32, kind="ExternalInput").ap()
    d_ident = nc.dram_tensor("ident", [128, 128], F32, kind="ExternalInput").ap()
    d_ones1 = nc.dram_tensor("ones1", [1, 128], F32, kind="ExternalInput").ap()
    d_onescol = nc.dram_tensor("onescol", [128, 1], BF16, kind="ExternalInput").ap()
    d_eps = nc.dram_tensor("epsc", [128, 1], F32, kind="ExternalInput").ap()

    sch = Sched()
    sch.__init_flush__()
    for s in ("dS", "dG", "tS", "vS", "sS", "gS", "oS"):
        sch.new_sem(s)

    import contextlib
    stack = contextlib.ExitStack()

    def sb(name, shape, dt):
        return stack.enter_context(nc.sbuf_tensor(name, shape, dt))

    with stack:
        c_f1 = sb("c_f1", [64, 128], BF16)
        c_f2st1 = sb("c_f2st1", [128, 64 * 128], BF16)
        c_f2st2 = sb("c_f2st2", [128, 64 * 128], BF16)
        c_e2048a = sb("c_e2048a", [128, 4], BF16)
        c_e2048b = sb("c_e2048b", [128, 4], BF16)
        c_fold2048 = sb("c_fold2048", [4, 2], F32)
        c_fold = sb("c_fold", [128, 64], BF16)
        c_winv = sb("c_winv", [128, 128], BF16)
        c_fcos = sb("c_fcos", [128, 64 * 64], BF16)
        c_fsin = sb("c_fsin", [128, 64 * 64], BF16)
        c_signmu = sb("c_signmu", [128, 64], F32)
        c_iotap = sb("c_iotap", [128, 1], F32)
        c_ident = sb("c_ident", [128, 128], F32)
        c_ones1 = sb("c_ones1", [1, 128], F32)
        c_onescol = sb("c_onescol", [128, 1], BF16)
        c_eps = sb("c_eps", [128, 1], F32)
        sv = sb("sv", [128, 8], F32)
        nsv = sb("nsv", [128, 8], F32)
        st_m = sb("st_m", [128, T], F32)
        st_istd = sb("st_istd", [128, T], F32)
        st_mistd = sb("st_mistd", [128, T], F32)
        st_i2 = sb("st_i2", [128, T], F32)
        st_tmp = sb("st_tmp", [128, T], F32)
        st_bias = sb("st_bias", [128, T], F32)
        colsum = sb("colsum", [128, 8], F32)
        vdum = sb("vdum", [1, 8], F32)
        gfence = sb("gfence", [128, 16], BF16)
        sdum = sb("sdum", [1, 8], F32)
        t1s = sb("t1s", [128, 8 * R], BF16)

        ps0 = stack.enter_context(nc.psum_tensor("ps0", [128, 512], F32))
        ps1 = stack.enter_context(nc.psum_tensor("ps1", [128, 512], F32))
        ps2 = stack.enter_context(nc.psum_tensor("ps2", [128, 1024], F32))
        ps3 = stack.enter_context(nc.psum_tensor("ps3", [128, 1024], F32))
        ps4 = stack.enter_context(nc.psum_tensor("ps4", [128, 512], F32))
        ps5 = stack.enter_context(nc.psum_tensor("ps5", [128, 512], F32))

        sem_objs = {}
        for s in ("dS", "dG", "tS", "vS", "sS", "gS", "oS"):
            sem_objs[s] = stack.enter_context(nc.semaphore(s))

        blk = stack.enter_context(nc.Block())

        def ld_sync(dst, src):
            return sch.op("sync", lambda e, d=dst, s=src: e.dma_start(out=d, in_=s), "dS", 16)

        sch.flush_fns = {"vector": lambda e: e.memset(vdum[:], 0.0),
                         "scalar": lambda e: e.memzero(sdum[:])}

        def flush_v():
            return sch.flush("vector")

        def flush_s():
            return sch.flush("scalar")

        for dst, src in [
            (c_f1[:], d_f1[:]), (c_f2st1[:], d_f2st1[:]), (c_f2st2[:], d_f2st2[:]),
            (c_e2048a[:], d_e2048a[:]), (c_e2048b[:], d_e2048b[:]),
            (c_fold2048[:], d_fold2048[:]), (c_fold[:], d_fold[:]),
            (c_winv[:], d_winv[:]), (c_fcos[:], d_fcos[:]), (c_fsin[:], d_fsin[:]),
            (c_signmu[:], d_signmu[:]), (c_iotap[:], d_iotap[:]),
            (c_ident[:], d_ident[:]), (c_ones1[:], d_ones1[:]),
            (c_onescol[:], d_onescol[:]),
            (c_eps[:], d_eps[:]),
            (sv[:], d_s[:]),
        ]:
            ld_sync(dst, src)
        consts_dS = sch.cnt("dS")
        sch.wait("vector", "dS", consts_dS)
        sch.op("vector", lambda e: e.tensor_scalar(out=nsv[:], in0=sv[:], scalar1=-1.0, scalar2=None, op0=ALU.mult), "vS")

        stL = contextlib.ExitStack()
        with stL:
            xrt = stL.enter_context(nc.sbuf_tensor("xrt", [128, L], F32))
            sqt = stL.enter_context(nc.sbuf_tensor("sqt", [128, L], F32))
            xbt = stL.enter_context(nc.sbuf_tensor("xbt", [128, L], BF16))
            for t in range(T):
                dS_x = ld_sync(xrt[:], d_xr[t * 128: t * 128 + 128, :])
                sch.wait("scalar", "dS", dS_x)
                sch.op("scalar", lambda e: e.activation(out=sqt[:], in_=xrt[:], func=AF.Square), "sS")
                sch.op("scalar", lambda e: e.copy(out=xbt[:], in_=xrt[:]), "sS")
                sS_sq = flush_s()
                sch.wait("vector", "sS", sS_sq)
                sch.op("vector", lambda e, t=t: e.reduce_sum(out=st_m[:, t:t+1], in_=xrt[:], axis=AX.X), "vS")
                vS_r = sch.op("vector", lambda e, t=t: e.reduce_sum(out=st_istd[:, t:t+1], in_=sqt[:], axis=AX.X), "vS")
                sch.wait("vector", "vS", vS_r)
                sch.op("vector", lambda e, t=t: e.tensor_scalar(out=st_m[:, t:t+1], in0=st_m[:, t:t+1], scalar1=1.0/L, scalar2=None, op0=ALU.mult), "vS")
                v0 = sch.op("vector", lambda e, t=t: e.tensor_scalar(out=st_istd[:, t:t+1], in0=st_istd[:, t:t+1], scalar1=1.0/L, scalar2=None, op0=ALU.mult), "vS")
                sch.wait("vector", "vS", v0)
                v1 = sch.op("vector", lambda e, t=t: e.tensor_tensor(out=st_tmp[:, t:t+1], in0=st_m[:, t:t+1], in1=st_m[:, t:t+1], op=ALU.mult), "vS")
                sch.wait("vector", "vS", v1)
                v2 = sch.op("vector", lambda e, t=t: e.tensor_tensor(out=st_istd[:, t:t+1], in0=st_istd[:, t:t+1], in1=st_tmp[:, t:t+1], op=ALU.subtract), "vS")
                sch.wait("scalar", "vS", v2)
                sch.op("scalar", lambda e, t=t: e.activation(out=st_istd[:, t:t+1], in_=st_istd[:, t:t+1], func=AF.Ln, bias=c_eps[:, 0:1]), "sS")
                sS_i = sch.op("scalar", lambda e, t=t: e.activation(out=st_istd[:, t:t+1], in_=st_istd[:, t:t+1], func=AF.Exp, scale=-0.5), "sS")
                sch.wait("vector", "sS", sS_i)
                sch.op("vector", lambda e, t=t: e.tensor_tensor(out=st_mistd[:, t:t+1], in0=st_m[:, t:t+1], in1=st_istd[:, t:t+1], op=ALU.mult), "vS")
                v3 = sch.op("vector", lambda e, t=t: e.tensor_tensor(out=st_i2[:, t:t+1], in0=st_istd[:, t:t+1], in1=st_istd[:, t:t+1], op=ALU.mult), "vS")
                sch.wait("vector", "vS", v3)
                v4 = sch.op("vector", lambda e, t=t: e.tensor_tensor(out=st_bias[:, t:t+1], in0=st_mistd[:, t:t+1], in1=st_mistd[:, t:t+1], op=ALU.mult), "vS")
                sch.wait("vector", "vS", v4)
                sch.op("vector", lambda e, t=t: e.tensor_scalar(out=st_bias[:, t:t+1], in0=st_bias[:, t:t+1], scalar1=-4096.0, scalar2=None, op0=ALU.mult), "vS")
                sch.wait("sync", "sS", sS_sq)
                sch.op("sync", lambda e, t=t: e.dma_start(out=d_xdup[t*128:(t+1)*128, 0:L], in_=xbt[:]), "dS", 16)
                sch.op("sync", lambda e, t=t: e.dma_start(out=d_xdup[t*128:(t+1)*128, L:2*L], in_=xbt[:]), "dS", 16)
                sch.wait("sync", "vS", vS_r)
        xdup_dS = sch.cnt("dS")
        stats_vS = sch.cnt("vS")

        stA = contextlib.ExitStack()
        with stA:
            vsb = stA.enter_context(nc.sbuf_tensor("vsb", [128, 32 * RANK], BF16))
            xtb = stA.enter_context(nc.sbuf_tensor("xtb", [128, 1024], BF16))
            istd_b = stA.enter_context(nc.sbuf_tensor("istd_b", [128, R], F32))
            mistd_b = stA.enter_context(nc.sbuf_tensor("mistd_b", [128, R], F32))
            rowv = stA.enter_context(nc.sbuf_tensor("rowv", [1, R], F32))
            rowv2 = stA.enter_context(nc.sbuf_tensor("rowv2", [1, R], F32))
            mm1u = stA.enter_context(nc.sbuf_tensor("mm1u", [128, 512], F32))
            mm1v = stA.enter_context(nc.sbuf_tensor("mm1v", [128, 512], F32))
            sch.wait("tensor", "vS", stats_vS)
            sch.wait("tensor", "dS", consts_dS)
            for t in range(T):
                sch.op("tensor", lambda e, t=t: e.matmul(ps2[0:1, t*128:(t+1)*128], st_istd[:, t:t+1], c_ident[:], start=True, stop=True), "tS")
                sch.op("tensor", lambda e, t=t: e.matmul(ps3[0:1, t*128:(t+1)*128], st_mistd[:, t:t+1], c_ident[:], start=True, stop=True), "tS")
            tS_rv = sch.cnt("tS")
            sch.wait("scalar", "tS", tS_rv)
            sch.op("scalar", lambda e: e.copy(out=rowv[:], in_=ps2[0:1, :]), "sS")
            sS_rv = sch.op("scalar", lambda e: e.copy(out=rowv2[:], in_=ps3[0:1, :]), "sS")
            sch.wait("tensor", "sS", sS_rv)
            for h in range(2):
                tS_b = sch.op("tensor", lambda e, h=h: e.matmul(ps0[:], c_ones1[:], rowv[0:1, h*512:(h+1)*512], start=True, stop=True), "tS")
                sch.wait("scalar", "tS", tS_b)
                sS_c = sch.op("scalar", lambda e, h=h: e.copy(out=istd_b[:, h*512:(h+1)*512], in_=ps0[:]), "sS")
                sch.wait("tensor", "sS", sS_c)
                tS_b2 = sch.op("tensor", lambda e, h=h: e.matmul(ps0[:], c_ones1[:], rowv2[0:1, h*512:(h+1)*512], start=True, stop=True), "tS")
                sch.wait("scalar", "tS", tS_b2)
                sS_c2 = sch.op("scalar", lambda e, h=h: e.copy(out=mistd_b[:, h*512:(h+1)*512], in_=ps0[:]), "sS")
                sch.wait("tensor", "sS", sS_c2)
            sch.wait("gpsimd", "vS", stats_vS)
            sch.wait("gpsimd", "dS", xdup_dS)
            sch.wait("gpsimd", "sS", sch.cnt("sS"))
            for k in range(32):
                sch.op("gpsimd", lambda e, k=k: e.dma_start(out=vsb[:, k*RANK:(k+1)*RANK], in_=d_v[k*128:(k+1)*128, :]), "dG", 16)
            dG_v = sch.cnt("dG")
            sch.wait("tensor", "dG", dG_v)
            for rs in range(8):
                for k in range(32):
                    sch.op("tensor", lambda e, k=k, rs=rs: e.matmul(
                        ps4[:, rs:rs+1], vsb[:, k*RANK+rs*128: k*RANK+(rs+1)*128],
                        c_onescol[:], start=(k == 0), stop=(k == 31)), "tS")
            tS_cs = sch.cnt("tS")
            sch.wait("scalar", "tS", tS_cs)
            sS_cs = sch.op("scalar", lambda e: e.copy(out=colsum[:], in_=ps4[:, 0:8]), "sS")
            sch.wait("tensor", "sS", sS_cs)

            psA = [ps0[:, :], ps1[:, :], ps2[:, 0:512], ps2[:, 512:1024],
                   ps3[:, 0:512], ps3[:, 512:1024], ps4[:, :], ps5[:, :]]
            if DEBUG:
                sch.wait("sync", "dG", sch.cnt("dG"))
                sch.op("sync", lambda e: e.dma_start(out=d_dbg_vsb[:], in_=vsb[:, 0:1024]), "oS", 16)
                sch.wait("sync", "sS", sch.cnt("sS"))
                sch.op("sync", lambda e: e.dma_start(out=d_dbg_ib[:], in_=istd_b[:]), "oS", 16)
                sch.op("sync", lambda e: e.dma_start(out=d_dbg_cs[:], in_=colsum[:]), "oS", 16)
            for half in range(2):
                sch.barrier()
                evac_done_vS = sch.cnt("vS")
                tS_chunk = {}
                for k in range(32):
                    if k >= 2:
                        sch.wait("gpsimd", "tS", tS_chunk[k - 2])
                    dG_x = sch.op("gpsimd", lambda e, k=k, half=half: e.dma_start(
                        out=xtb[:, (k % 2)*512:(k % 2)*512+512],
                        in_=d_xt[k*128:(k+1)*128, half*512:(half+1)*512]), "dG", 16)
                    sch.wait("tensor", "dG", dG_x)
                    if k == 0:
                        sch.wait("tensor", "vS", evac_done_vS)
                    for rs in range(8):
                        sch.op("tensor", lambda e, k=k, rs=rs: e.matmul(
                            psA[rs], vsb[:, k*RANK+rs*128: k*RANK+(rs+1)*128],
                            xtb[:, (k % 2)*512:(k % 2)*512+512],
                            start=(k == 0), stop=(k == 31)), "tS")
                    tS_chunk[k] = sch.cnt("tS")
                tS_mm1 = sch.cnt("tS")
                if DEBUG and half == 0:
                    sch.wait("scalar", "tS", tS_mm1)
                    sch.op("scalar", lambda e: e.copy(out=mm1u[:], in_=psA[0]), "sS")
                    sS_d = sch.cnt("sS")
                    sch.wait("tensor", "sS", sS_d)
                    sch.wait("sync", "sS", sS_d)
                    sch.op("sync", lambda e: e.dma_start(out=d_dbg_ps[:], in_=mm1u[:]), "oS", 16)
                    sch.wait("vector", "oS", sch.cnt("oS"))
                sch.wait("vector", "tS", tS_mm1)
                sch.wait("vector", "sS", sch.cnt("sS"))
                for rs in range(8):
                    v_a = sch.op("vector", lambda e, rs=rs, half=half: e.tensor_tensor(
                        out=mm1u[:], in0=psA[rs], in1=istd_b[:, half*512:(half+1)*512], op=ALU.mult), "vS")
                    sch.wait("vector", "vS", v_a)
                    v_b = sch.op("vector", lambda e, rs=rs, half=half: e.scalar_tensor_tensor(
                        out=mm1v[:], in0=mistd_b[:, half*512:(half+1)*512],
                        scalar=colsum[:, rs:rs+1], in1=mm1u[:], op0=ALU.mult, op1=ALU.subtract), "vS")
                    sch.wait("vector", "vS", v_b)
                    sch.op("vector", lambda e, rs=rs, half=half: e.tensor_scalar(
                        out=t1s[:, rs*R+half*512: rs*R+(half+1)*512], in0=mm1v[:],
                        scalar1=nsv[:, rs:rs+1], scalar2=None, op0=ALU.mult), "vS")

        flush_v()
        stB = contextlib.ExitStack()
        with stB:
            utb = stB.enter_context(nc.sbuf_tensor("utb", [128, 8 * L], BF16))
            gbuf = stB.enter_context(nc.sbuf_tensor("gbuf", [128, 2048], BF16))
            sch.wait("gpsimd", "tS", sch.cnt("tS"))
            sch.wait("gpsimd", "vS", sch.cnt("vS"))
            sch.wait("gpsimd", "sS", sch.cnt("sS"))
            sch.wait("gpsimd", "oS", sch.cnt("oS"))
            for rc in range(8):
                sch.op("gpsimd", lambda e, rc=rc: e.dma_start(out=utb[:, rc*L:(rc+1)*L], in_=d_ut[rc*128:(rc+1)*128, :]), "dG", 16)
            dG_ut = sch.cnt("dG")
            sch.wait("tensor", "dG", dG_ut)
            sch.wait("tensor", "vS", sch.cnt("vS"))
            psB = [ps2[:, 0:512], ps2[:, 512:1024], ps3[:, 0:512], ps3[:, 512:1024]]
            for t in range(T):
                for og in range(2):
                    for rc in range(8):
                        for oc in range(4):
                            sch.op("tensor", lambda e, t=t, og=og, rc=rc, oc=oc: e.matmul(
                                psB[oc], t1s[:, rc*R+t*128: rc*R+(t+1)*128],
                                utb[:, rc*L+(og*4+oc)*512: rc*L+(og*4+oc+1)*512],
                                start=(rc == 0), stop=(rc == 7)), "tS")
                    tS_g = sch.cnt("tS")
                    sch.wait("scalar", "tS", tS_g)
                    for oc in range(4):
                        sch.op("scalar", lambda e, oc=oc: e.activation(out=gbuf[:, oc*512:(oc+1)*512], in_=psB[oc], func=AF.Silu), "sS")
                    sS_g = flush_s()
                    sch.wait("sync", "sS", sS_g)
                    dS_g = sch.op("sync", lambda e, t=t, og=og: e.dma_start(
                        out=d_gsp[t*128:(t+1)*128, og*2048:(og+1)*2048], in_=gbuf[:]), "dS", 16)
                    sch.wait("scalar", "dS", dS_g)
                    sch.wait("tensor", "sS", sS_g)

        if DEBUG:
            sch.op("sync", lambda e: e.dma_start(out=d_dbg_st[:, 0*T:1*T], in_=st_m[:]), "oS", 16)
            sch.op("sync", lambda e: e.dma_start(out=d_dbg_st[:, 1*T:2*T], in_=st_istd[:]), "oS", 16)
            sch.op("sync", lambda e: e.dma_start(out=d_dbg_st[:, 2*T:3*T], in_=st_i2[:]), "oS", 16)
            sch.op("sync", lambda e: e.dma_start(out=d_dbg_st[:, 3*T:4*T], in_=st_bias[:]), "oS", 16)
            sch.op("sync", lambda e: e.dma_start(out=d_dbg_st[:, 4*T:5*T], in_=st_mistd[:]), "oS", 16)
            sch.op("sync", lambda e: e.dma_start(out=d_dbg_t1[:], in_=t1s[:]), "oS", 16)

        stC = contextlib.ExitStack()
        with stC:
            def sbc(name, shape, dt):
                return stC.enter_context(nc.sbuf_tensor(name, shape, dt))
            xg = sbc("xg", [64, 128 * 64], BF16)
            big1 = sbc("big1", [128, 8192], BF16)
            psb_ = sbc("psb_", [128, 4096], BF16)
            sq64 = sbc("sq64", [128, 1024], BF16)
            scr16 = sbc("scr16", [128, 4096], F32)
            x2a = sbc("x2a", [4, 64], F32)
            x2b = sbc("x2b", [4, 64], F32)
            p2row = sbc("p2row", [2, 64], F32)
            p2048 = sbc("p2048", [128, 1], F32)
            p2e = sbc("p2e", [128, 1], F32)
            corr = sbc("corr", [128, 4096], F32)
            vals8 = sbc("vals8", [128, 8], F32)
            idx8 = sbc("idx8", [128, 8], U32)
            idxf = sbc("idxf", [128, 8], F32)
            w8 = sbc("w8", [128, 8], F32)
            w8n = sbc("w8n", [128, 8], F32)
            wsc = sbc("wsc", [128, 4], F32)
            sas = sbc("sas", [128, 2], F32)
            idxu = sbc("idxu", [128, 8], U32)
            sacc = sbc("sacc", [128, 4096], BF16)
            gout = sbc("gout", [128, 4096], BF16)
            xrt2 = sbc("xrt2", [128, 4096], F32)

            gsb = big1
            ysb = big1
            rollsv = scr16[:].bitcast(BF16)

            def init_breg(e):
                e._breg = e.alloc_register("breg")
                return e.reg_mov(e._breg, BOUND)
            sch.op("gpsimd", init_breg, "gS")

            for t in range(T):
                sch.barrier()
                dG_xg = sch.op("gpsimd", lambda e, t=t: e.dma_start(out=xg[:], in_=d_xf3[:, t*8192:(t+1)*8192]), "dG", 16)
                dS_x2 = ld_sync(xrt2[:], d_xr[t*128:(t+1)*128, :])
                dS_g2 = ld_sync(gout[:], d_gsp[t*128:(t+1)*128, :])

                sch.wait("tensor", "dG", dG_xg)


                f1_psums = [ps0, ps1]
                f1_evac = {}
                for grp in range(16):
                    pb = f1_psums[grp % 2]
                    if grp >= 2:
                        sch.wait("tensor", "sS", f1_evac[grp - 2])
                    for pi in range(4):
                        pair = grp * 4 + pi
                        sch.op("tensor", lambda e, pair=pair, pb=pb, pi=pi: e.matmul(
                            pb[:, pi*128:(pi+1)*128],
                            xg[:, pair*128:(pair+1)*128],
                            c_f1[:], start=True, stop=True), "tS")
                    tS_f1 = sch.cnt("tS")
                    sch.wait("scalar", "tS", tS_f1)
                    sch.op("scalar", lambda e, grp=grp, pb=pb: e.copy(
                        out=gsb[:, grp*512:(grp+1)*512], in_=pb[:]), "sS")
                    f1_evac[grp] = flush_s()
                sS_g_all = flush_s()
                sch.wait("tensor", "sS", sS_g_all)
                if DEBUG and t == 0:
                    sch.wait("sync", "sS", sS_g_all)
                    sch.op("sync", lambda e: e.dma_start(out=d_dbg_gsb[:], in_=big1[:]), "oS", 16)

                f2_psums = [ps2, ps3]
                sq_done = {}
                pevac_done = {}
                for ph in range(4):
                    pb = f2_psums[ph % 2]
                    if ph >= 2:
                        sch.wait("tensor", "sS", sq_done[ph - 2])
                    for din in range(16):
                        d = ph * 16 + din
                        sch.op("tensor", lambda e, d=d, din=din, pb=pb: e.matmul(
                            pb[:, din*64:(din+1)*64], c_f2st1[:, d*128:(d+1)*128],
                            gsb[:].rearrange("p (q f) -> p f q", q=64)[:, 0*64 + d, :],
                            start=True, stop=False), "tS")
                        sch.op("tensor", lambda e, d=d, din=din, pb=pb: e.matmul(
                            pb[:, din*64:(din+1)*64], c_f2st2[:, d*128:(d+1)*128],
                            gsb[:].rearrange("p (q f) -> p f q", q=64)[:, 64 + d, :],
                            start=False, stop=True), "tS")
                    if ph == 0:
                        sch.op("tensor", lambda e: e.matmul(ps4[0:4, 0:64], c_e2048a[:], gsb[:].rearrange("p (q f) -> p f q", q=64)[:, 0, :], start=True, stop=False), "tS")
                        sch.op("tensor", lambda e: e.matmul(ps4[0:4, 0:64], c_e2048b[:], gsb[:].rearrange("p (q f) -> p f q", q=64)[:, 64, :], start=False, stop=True), "tS")
                    tS_f2 = sch.cnt("tS")
                    sch.wait("scalar", "tS", tS_f2)
                    sch.op("scalar", lambda e, pb=pb: e.activation(out=sq64[0:64, :], in_=pb[0:64, :], func=AF.Square), "sS")
                    sch.op("scalar", lambda e, pb=pb: e.activation(out=sq64[64:128, :], in_=pb[64:128, :], func=AF.Square), "sS")
                    sq_done[ph] = flush_s()
                    sch.wait("tensor", "sS", sq_done[ph])
                    for nh in range(2):
                        tS_fo = sch.op("tensor", lambda e, nh=nh: e.matmul(
                            ps5[0:64, :], c_fold[:], sq64[:, nh*512:(nh+1)*512],
                            start=True, stop=True), "tS")
                        sch.wait("scalar", "tS", tS_fo)
                        sch.op("scalar", lambda e, ph=ph, nh=nh: e.copy(
                            out=psb_[0:32, ph*1024 + nh*512: ph*1024 + (nh+1)*512],
                            in_=ps5[0:32, :]), "sS")
                        sch.op("scalar", lambda e, ph=ph, nh=nh: e.copy(
                            out=psb_[64:96, ph*1024 + nh*512: ph*1024 + (nh+1)*512],
                            in_=ps5[32:64, :]), "sS")
                        sS_pe = flush_s()
                        pevac_done[(ph, nh)] = sS_pe
                        sch.wait("tensor", "sS", sS_pe)
                sS_pall = sch.cnt("sS")
                sch.wait("vector", "sS", sS_pall)
                sch.op("vector", lambda e: e.tensor_scalar(out=psb_[0:1, 0:64], in0=psb_[0:1, 0:64], scalar1=0.5, scalar2=None, op0=ALU.mult), "vS")
                sch.op("vector", lambda e: e.tensor_scalar(out=psb_[64:65, 0:64], in0=psb_[64:65, 0:64], scalar1=0.5, scalar2=None, op0=ALU.mult), "vS")
                vS_ph = flush_v()

                if DEBUG and t == 0:
                    sch.wait("sync", "vS", vS_ph)
                    sch.op("sync", lambda e: e.dma_start(out=d_dbg_psb[:], in_=psb_[:]), "oS", 16)

                tS_e = sch.cnt("tS")
                sch.wait("scalar", "tS", tS_e)
                sS_x2 = sch.op("scalar", lambda e: e.copy(out=x2a[:], in_=ps4[0:4, 0:64]), "sS")
                sch.wait("scalar", "sS", sS_x2)
                sS_x2b = sch.op("scalar", lambda e: e.activation(out=x2b[:], in_=x2a[:], func=AF.Square), "sS")
                sch.wait("tensor", "sS", sS_x2b)
                tS_p2 = sch.op("tensor", lambda e: e.matmul(ps4[0:2, 64:128], c_fold2048[:], x2b[:], start=True, stop=True), "tS")
                sch.wait("scalar", "tS", tS_p2)
                sS_p2r = sch.op("scalar", lambda e: e.copy(out=p2row[:], in_=ps4[0:2, 64:128]), "sS")
                sch.wait("tensor", "sS", sS_p2r)
                tS_p2t = sch.op("tensor", lambda e: e.matmul(ps4[0:64, 128:130], p2row[:], c_ident[0:2, 0:2], start=True, stop=True), "tS")
                sch.wait("scalar", "tS", tS_p2t)
                sch.op("scalar", lambda e: e.copy(out=p2048[0:64, :], in_=ps4[0:64, 128:129]), "sS")
                sS_p2c = sch.op("scalar", lambda e: e.copy(out=p2048[64:128, :], in_=ps4[0:64, 129:130]), "sS")
                sch.wait("vector", "sS", sS_p2c)
                v_p = sch.op("vector", lambda e, t=t: e.tensor_tensor(out=p2e[:], in0=p2048[:], in1=st_i2[:, t:t+1], op=ALU.mult), "vS")
                sch.wait("vector", "vS", v_p)
                vS_p2e = sch.op("vector", lambda e: e.tensor_scalar(out=p2e[:], in0=p2e[:], scalar1=1.0/4096.0, scalar2=None, op0=ALU.mult), "vS")

                sch.wait("tensor", "vS", vS_ph)
                if DEBUG and t == 0:
                    sch.wait("scalar", "oS", sch.cnt("oS"))
                psb3 = psb_[:].rearrange("p (d q) -> p d q", d=64)
                i1_psums = [ps2, ps3]
                i1_evac = {}
                for g in range(8):
                    pb = i1_psums[g % 2]
                    if g >= 2:
                        sch.wait("tensor", "sS", i1_evac[g - 2])
                    for slot in range(8):
                        pr = g * 8 + slot
                        for r2 in range(2):
                            sch.op("tensor", lambda e, pr=pr, slot=slot, r2=r2, pb=pb: e.matmul(
                                pb[64*r2:64*r2+64, slot*128:(slot+1)*128],
                                psb3[64*r2:64*r2+32, :, pr],
                                c_winv[64*r2:64*r2+32, :],
                                start=True, stop=True), "tS")
                    tS_i1 = sch.cnt("tS")
                    sch.wait("scalar", "tS", tS_i1)
                    sch.op("scalar", lambda e, g=g, pb=pb: e.copy(
                        out=ysb[:, g*1024:(g+1)*1024], in_=pb[:]), "sS")
                    i1_evac[g] = flush_s()
                sch.wait("tensor", "sS", sch.cnt("sS"))

                if DEBUG and t == 0:
                    sch.wait("sync", "sS", sch.cnt("sS"))
                    sch.op("sync", lambda e: e.dma_start(out=d_dbg_ysb[:], in_=big1[:]), "oS", 16)


                i2_psums = [ps2, ps3]
                i2_evac = {}
                for ph in range(4):
                    pb = i2_psums[ph % 2]
                    if ph >= 2:
                        sch.wait("tensor", "sS", i2_evac[ph - 2])
                    for min_ in range(16):
                        mu = ph * 16 + min_
                        for r2 in range(2):
                            sch.op("tensor", lambda e, mu=mu, min_=min_, r2=r2, pb=pb: e.matmul(
                                pb[64*r2:64*r2+64, min_*64:(min_+1)*64],
                                ysb[64*r2:64*r2+64, :].rearrange("p (r f) -> p f r", r=64)[:, 0*64 + mu, :],
                                c_fcos[64*r2:64*r2+64, mu*64:(mu+1)*64],
                                start=True, stop=False), "tS")
                            sch.op("tensor", lambda e, mu=mu, min_=min_, r2=r2, pb=pb: e.matmul(
                                pb[64*r2:64*r2+64, min_*64:(min_+1)*64],
                                ysb[64*r2:64*r2+64, :].rearrange("p (r f) -> p f r", r=64)[:, 64 + mu, :],
                                c_fsin[64*r2:64*r2+64, mu*64:(mu+1)*64],
                                start=False, stop=True), "tS")
                    tS_i2 = sch.cnt("tS")
                    sch.wait("scalar", "tS", tS_i2)
                    sch.op("scalar", lambda e, ph=ph, pb=pb, t=t: e.activation(
                        out=corr[:].rearrange("p (lam mu) -> p mu lam", mu=64)[:, ph*16:(ph+1)*16, :],
                        in_=pb[:].rearrange("p (mi lam) -> p mi lam", mi=16),
                        func=AF.Identity, bias=st_bias[:, t:t+1], scale=st_i2[:, t:t+1]), "sS")
                    i2_evac[ph] = flush_s()
                sS_corr = sch.cnt("sS")

                sch.wait("vector", "sS", sS_corr)
                sch.wait("vector", "vS", vS_p2e)
                corr3 = corr[:].rearrange("p (lam mu) -> p lam mu", mu=64)
                sgb = c_signmu[:].unsqueeze(1).broadcast_to((128, 64, 64))
                vS_ed = sch.op("vector", lambda e: e.scalar_tensor_tensor(
                    out=corr3, in0=sgb, scalar=p2e[:, 0:1], in1=corr3,
                    op0=ALU.mult, op1=ALU.add), "vS")

                sch.wait("vector", "vS", vS_ed)
                vS_mx = sch.op("vector", lambda e: e.max(out=vals8[:], in_=corr[:]), "vS")
                sch.wait("vector", "vS", vS_mx)
                vS_ix = sch.op("vector", lambda e: e.max_index(out=idx8[:], in_max=vals8[:], in_values=corr[:]), "vS")
                sch.wait("vector", "vS", vS_ix)
                vS_v0 = sch.op("vector", lambda e: e.tensor_scalar(out=wsc[:, 0:1], in0=vals8[:, 0:1], scalar1=-1.0, scalar2=None, op0=ALU.mult), "vS")
                sch.wait("scalar", "vS", vS_v0)
                sS_exp = sch.op("scalar", lambda e: e.activation(out=w8[:], in_=vals8[:], func=AF.Exp, bias=wsc[:, 0:1]), "sS")
                sch.wait("vector", "sS", sS_exp)
                vS_sw = sch.op("vector", lambda e: e.reduce_sum(out=wsc[:, 1:2], in_=w8[:], axis=AX.X), "vS")
                sch.wait("vector", "vS", vS_sw)
                vS_rw = sch.op("vector", lambda e: e.reciprocal(out=wsc[:, 2:3], in_=wsc[:, 1:2]), "vS")
                sch.wait("vector", "vS", vS_rw)
                vS_wn = sch.op("vector", lambda e: e.tensor_scalar(out=w8n[:], in0=w8[:], scalar1=wsc[:, 2:3], scalar2=None, op0=ALU.mult), "vS")
                sch.wait("vector", "vS", vS_wn)
                vS_sw2 = sch.op("vector", lambda e: e.reduce_sum(out=wsc[:, 3:4], in_=w8n[:], axis=AX.X), "vS")
                sch.op("vector", lambda e: e.tensor_copy(out=idxf[:], in_=idx8[:]), "vS")
                vS_if = sch.op("vector", lambda e, t=t: e.tensor_scalar(
                    out=idxf[:], in0=idxf[:], scalar1=c_iotap[:, 0:1], scalar2=float(t*128*8192), op0=ALU.add, op1=ALU.add), "vS")
                sch.wait("vector", "vS", vS_if)
                vS_msk = sch.op("vector", lambda e: e.tensor_scalar(
                    out=w8[:], in0=w8n[:], scalar1=0.0, scalar2=OOB_OFF, op0=ALU.is_equal, op1=ALU.mult), "vS")
                sch.wait("vector", "vS", vS_msk)
                vS_ia = sch.op("vector", lambda e: e.tensor_tensor(out=idxf[:], in0=idxf[:], in1=w8[:], op=ALU.add), "vS")
                sch.wait("vector", "vS", vS_ia)
                sch.op("vector", lambda e: e.tensor_copy(out=idxu[:], in_=idxf[:]), "vS")
                vS_iu = flush_v()
                sch.wait("vector", "vS", vS_sw2)
                v_s1 = sch.op("vector", lambda e, t=t: e.tensor_tensor(out=sas[:, 1:2], in0=st_mistd[:, t:t+1], in1=wsc[:, 3:4], op=ALU.mult), "vS")
                sch.wait("vector", "vS", v_s1)
                vS_sa = sch.op("vector", lambda e: e.tensor_scalar(out=sas[:, 1:2], in0=sas[:, 1:2], scalar1=-1.0, scalar2=None, op0=ALU.mult), "vS")

                sch.op("vector", lambda e: e.memset(rollsv[:, 0:4096], 0.0), "vS")
                sch.op("vector", lambda e: e.memset(rollsv[:, 4096:8192], 0.0), "vS")
                vS_mz = flush_v()
                sch.wait("gpsimd", "vS", vS_iu)
                sch.wait("gpsimd", "vS", vS_mz)
                sch.wait("gpsimd", "dS", xdup_dS)
                s_steps = {}
                for kk in range(8):
                    rb = rollsv[:, (kk % 2)*4096:(kk % 2)*4096+4096]
                    if kk >= 2:
                        sch.wait("gpsimd", "vS", s_steps[kk - 2])
                    sch.op("gpsimd", lambda e, kk=kk, rb=rb: e.indirect_dma_start(
                        rb, None, d_xdup[:], IndirectOffsetOnAxis(idxu[:, kk:kk+1], 1),
                        bounds_check=e._breg, oob_is_err=False), "dG", 16)
                    dG_gk = sch.op("gpsimd", lambda e, kk=kk: e.indirect_dma_start(
                        psb_[:], None, d_xdup[:], IndirectOffsetOnAxis(idxu[:, kk:kk+1], 1),
                        bounds_check=e._breg, oob_is_err=False), "dG", 16)
                    sch.wait("vector", "dG", dG_gk)
                    if kk == 0:
                        s_steps[0] = sch.op("vector", lambda e, rb=rb: e.tensor_scalar(
                            out=sacc[:], in0=rb, scalar1=w8n[:, 0:1], scalar2=None, op0=ALU.mult), "vS")
                    else:
                        sch.wait("vector", "vS", s_steps[kk - 1])
                        s_steps[kk] = sch.op("vector", lambda e, kk=kk, rb=rb: e.scalar_tensor_tensor(
                            out=sacc[:], in0=rb, scalar=w8n[:, kk:kk+1], in1=sacc[:], op0=ALU.mult, op1=ALU.add), "vS")
                vS_s = s_steps[7]

                sch.wait("vector", "vS", vS_s)
                sch.wait("vector", "vS", vS_sa)
                vS_u = sch.op("vector", lambda e, t=t: e.tensor_scalar(
                    out=sacc[:], in0=sacc[:], scalar1=st_istd[:, t:t+1], scalar2=sas[:, 1:2], op0=ALU.mult, op1=ALU.add), "vS")
                sch.wait("vector", "dS", dS_g2)
                sch.wait("vector", "vS", vS_u)
                vS_vv = sch.op("vector", lambda e: e.tensor_tensor(out=scr16[:], in0=sacc[:], in1=gout[:], op=ALU.mult), "vS")
                sch.wait("vector", "dS", dS_x2)
                sch.wait("vector", "vS", vS_vv)
                vS_o = sch.op("vector", lambda e: e.tensor_tensor(out=scr16[:], in0=scr16[:], in1=xrt2[:], op=ALU.add), "vS")
                if DEBUG and t == 0:
                    sch.wait("sync", "vS", vS_ed)
                    sch.op("sync", lambda e: e.dma_start(out=d_dbg_corr[:], in_=corr[:]), "oS", 16)
                    sch.wait("sync", "vS", vS_ix)
                    sch.op("sync", lambda e: e.dma_start(out=d_dbg_v8[:], in_=vals8[:]), "oS", 16)
                    sch.op("sync", lambda e: e.dma_start(out=d_dbg_i8[:], in_=idx8[:]), "oS", 16)
                    sch.wait("sync", "vS", vS_wn)
                    sch.op("sync", lambda e: e.dma_start(out=d_dbg_w8[:], in_=w8n[:]), "oS", 16)
                    sch.wait("sync", "dS", dS_g2)
                    sch.op("sync", lambda e: e.dma_start(out=d_dbg_g[:], in_=gout[:]), "oS", 16)
                    sch.wait("sync", "vS", vS_u)
                    sch.op("sync", lambda e: e.dma_start(out=d_dbg_s[:], in_=sacc[:]), "oS", 16)
                sch.wait("sync", "vS", vS_o)
                sch.op("sync", lambda e, t=t: e.dma_start(out=d_out[t*128:(t+1)*128, :], in_=scr16[:]), "oS", 16)

            sch.wait("sync", "oS", sch.cnt("oS"))

        @blk.sync
        def _(sync):
            sch.emit("sync", sync, sem_objs)

        @blk.vector
        def _(vector):
            sch.emit("vector", vector, sem_objs)

        @blk.scalar
        def _(scalar):
            sch.emit("scalar", scalar, sem_objs)

        @blk.tensor
        def _(tensor):
            sch.emit("tensor", tensor, sem_objs)

        @blk.gpsimd
        def _(gpsimd):
            sch.emit("gpsimd", gpsimd, sem_objs)

    return nc


def kernel(L0_attn_x, L0_ffn_U, L0_ffn_S, L0_ffn_V):
    x = np.ascontiguousarray(np.asarray(L0_attn_x, dtype=np.float32).reshape(8192, L))
    U = np.asarray(L0_ffn_U, dtype=np.float32)
    S = np.asarray(L0_ffn_S, dtype=np.float32)
    V = np.asarray(L0_ffn_V, dtype=np.float32)
    UT = np.ascontiguousarray(U.T)
    c = build_consts()
    nc = build_program()
    in_maps = []
    for ci in range(N_CORES):
        rows = slice(ci * R, (ci + 1) * R)
        xr = x[rows]
        xf3 = np.ascontiguousarray(xr.reshape(T, 2, 64, 64, 64).transpose(3, 0, 2, 1, 4)).reshape(64, R * 64)
        xt = np.ascontiguousarray(xr.T)
        in_maps.append({
            "xr": xr, "xf3": xf3, "xt": xt, "vw": V, "ut": UT,
            "sw": np.ascontiguousarray(S.reshape(8, 128).T),
            "f1const": c["f1const"], "f2st1": c["f2st1"], "f2st2": c["f2st2"],
            "e2048a": c["e2048a"], "e2048b": c["e2048b"], "fold2048": c["fold2048"],
            "fold": c["fold"], "winv": c["winv"], "fcos": c["fcos"], "fsin": c["fsin"],
            "signmu": c["signmu"], "iotap": c["iotap"], "ident": c["ident"],
            "ones1": c["ones1"], "onescol": c["onescol"], "epsc": c["epsc"],
        })
    res = run_bass_kernel_spmd(nc, in_maps, core_ids=list(range(N_CORES)))
    out = np.concatenate([r["out"] for r in res.results], axis=0)
    return out.reshape(2, 4096, 4096)


# version 10
# speedup vs baseline: 1.0111x; 1.0111x over previous
import numpy as np
import ml_dtypes
import concourse.bass as bass
import concourse.mybir as mybir
from concourse.bass import IndirectOffsetOnAxis
from concourse.bass_utils import run_bass_kernel_spmd

F32 = mybir.dt.float32
BF16 = mybir.dt.bfloat16
U32 = mybir.dt.uint32
AF = mybir.ActivationFunctionType
ALU = mybir.AluOpType
AX = mybir.AxisListType

N_CORES = 8
R = 1024
T = 8
L = 4096
RANK = 1024
OOB_OFF = 8388608.0
BOUND = 8388607
BF = ml_dtypes.bfloat16
DEBUG = False

_CACHE = {}


def build_consts():
    if _CACHE:
        return _CACHE
    c = {}
    a64 = np.arange(64)
    c32 = np.arange(32)
    mu64 = np.arange(64)

    th = 2 * np.pi * np.outer(a64, a64) / 64.0
    f1 = np.zeros((64, 128), np.float32)
    f1[:, 0:64] = np.cos(th)
    f1[:, 64:128] = -np.sin(th)
    c["f1const"] = f1.astype(BF)

    st1 = np.zeros((128, 64, 128), np.float32)
    st2 = np.zeros((128, 64, 128), np.float32)
    for d in range(64):
        k = 64 * c32 + d
        t2 = 2 * np.pi * np.outer(a64, k) / 4096.0
        cosv, sinv = np.cos(t2), np.sin(t2)
        for r2 in range(2):
            rs = slice(64 * r2, 64 * r2 + 64)
            st1[rs, d, 0 + 32 * r2: 32 + 32 * r2] = cosv
            st1[rs, d, 64 + 32 * r2: 96 + 32 * r2] = -sinv
            st2[rs, d, 0 + 32 * r2: 32 + 32 * r2] = sinv
            st2[rs, d, 64 + 32 * r2: 96 + 32 * r2] = cosv
    c["f2st1"] = st1.reshape(128, 64 * 128).astype(BF)
    c["f2st2"] = st2.reshape(128, 64 * 128).astype(BF)

    e1 = np.zeros((128, 4), np.float32)
    e2 = np.zeros((128, 4), np.float32)
    sgn = ((-1.0) ** a64).astype(np.float32)
    for r2 in range(2):
        e1[64 * r2: 64 * r2 + 64, 0 + r2] = sgn
        e2[64 * r2: 64 * r2 + 64, 2 + r2] = sgn
    c["e2048a"] = e1.astype(BF)
    c["e2048b"] = e2.astype(BF)
    f2048 = np.zeros((4, 2), np.float32)
    f2048[0, 0] = f2048[2, 0] = 1.0
    f2048[1, 1] = f2048[3, 1] = 1.0
    c["fold2048"] = f2048

    fold = np.zeros((128, 64), np.float32)
    for csx in range(2):
        for r2 in range(2):
            for ci in range(32):
                fold[csx * 64 + r2 * 32 + ci, r2 * 32 + ci] = 1.0
    c["fold"] = fold.astype(BF)

    thw = 2 * np.pi * np.outer(c32, mu64) / 64.0
    wv = np.zeros((32, 128), np.float32)
    wv[:, 0:64] = np.cos(thw)
    wv[:, 64:128] = np.sin(thw)
    winv = np.zeros((128, 128), np.float32)
    winv[0:32] = wv
    winv[64:96] = wv
    c["winv"] = winv.astype(BF)

    lam = np.arange(64)
    fcos = np.zeros((128, 64, 64), np.float32)
    fsin = np.zeros((128, 64, 64), np.float32)
    for m_ in range(64):
        lv = 64 * lam + m_
        tb = 2 * np.pi * np.outer(a64, lv) / 4096.0
        fc = (2.0 / 4096.0) * np.cos(tb)
        fs = (-2.0 / 4096.0) * np.sin(tb)
        fcos[0:64, m_] = fc
        fcos[64:128, m_] = fc
        fsin[0:64, m_] = fs
        fsin[64:128, m_] = fs
    c["fcos"] = fcos.reshape(128, 64 * 64).astype(BF)
    c["fsin"] = fsin.reshape(128, 64 * 64).astype(BF)

    c["signmu"] = np.tile(((-1.0) ** mu64).astype(np.float32), (128, 1))
    c["iotap"] = (np.arange(128, dtype=np.float32) * 8192.0).reshape(128, 1)
    c["ident"] = np.eye(128, dtype=np.float32)
    c["ones1"] = np.ones((1, 128), np.float32)
    c["epsc"] = np.full((128, 1), 1e-5, np.float32)
    c["onescol"] = np.ones((128, 1), np.float32).astype(BF)
    _CACHE.update(c)
    return c


class Sched:
    ENGINES = ("sync", "vector", "scalar", "tensor", "gpsimd")

    def __init__(self):
        self.script = {e: [] for e in self.ENGINES}
        self.counts = {}

    def new_sem(self, name):
        self.counts[name] = 0

    def wait(self, eng, sem, cnt=None):
        owner = self.SEM_OWNER.get(sem)
        if (owner is not None and owner != eng
                and (cnt is None or cnt >= self.counts[sem])
                and getattr(self, "flush_fns", None)):
            cnt = self.flush(owner)
        c = self.counts[sem] if cnt is None else cnt
        if c > 0:
            self.script[eng].append(("wait", sem, c))

    SELF_SEM = {"vector": "vS", "scalar": "sS"}
    SEM_OWNER = {"vS": "vector", "sS": "scalar"}

    def __init_flush__(self):
        self.flush_fns = {}
        self.last_was_flush = {"vector": False, "scalar": False}

    def op(self, eng, fn, inc=None, inc_n=1, _is_flush=False):
        own = self.SELF_SEM.get(eng)
        if own is not None and self.counts.get(own, 0) > 0:
            self.script[eng].append(("wait", own, self.counts[own]))
        self.script[eng].append(("op", fn, inc, inc_n))
        if eng in self.last_was_flush:
            self.last_was_flush[eng] = _is_flush
        if inc is not None:
            self.counts[inc] += inc_n
            return self.counts[inc]
        return None

    def flush(self, owner):
        if self.last_was_flush.get(owner):
            return self.counts[self.SELF_SEM[owner]]
        return self.op(owner, self.flush_fns[owner], self.SELF_SEM[owner], _is_flush=True)

    def cnt(self, sem):
        return self.counts[sem]

    def barrier(self):
        snap = dict(self.counts)
        for eng in self.ENGINES:
            for sem, c in snap.items():
                self.wait(eng, sem, c)

    def emit(self, eng_name, eng, sems):
        for item in self.script[eng_name]:
            if item[0] == "wait":
                eng.wait_ge(sems[item[1]], item[2])
            else:
                _, fn, inc, inc_n = item
                instr = fn(eng)
                if inc is not None:
                    instr.then_inc(sems[inc], inc_n)


def build_program():
    nc = bass.Bass("TRN2", target_bir_lowering=False)

    d_xr = nc.dram_tensor("xr", [R, L], F32, kind="ExternalInput").ap()
    d_xf3 = nc.dram_tensor("xf3", [64, R * 64], F32, kind="ExternalInput").ap()
    d_xt = nc.dram_tensor("xt", [L, R], F32, kind="ExternalInput").ap()
    d_v = nc.dram_tensor("vw", [L, RANK], F32, kind="ExternalInput").ap()
    d_ut = nc.dram_tensor("ut", [RANK, L], F32, kind="ExternalInput").ap()
    d_s = nc.dram_tensor("sw", [128, 8], F32, kind="ExternalInput").ap()
    d_out = nc.dram_tensor("out", [R, L], F32, kind="ExternalOutput").ap()
    if DEBUG:
        d_dbg_st = nc.dram_tensor("dbg_st", [128, 6 * T], F32, kind="ExternalOutput").ap()
        d_dbg_corr = nc.dram_tensor("dbg_corr", [128, L], F32, kind="ExternalOutput").ap()
        d_dbg_v8 = nc.dram_tensor("dbg_v8", [128, 8], F32, kind="ExternalOutput").ap()
        d_dbg_i8 = nc.dram_tensor("dbg_i8", [128, 8], U32, kind="ExternalOutput").ap()
        d_dbg_w8 = nc.dram_tensor("dbg_w8", [128, 8], F32, kind="ExternalOutput").ap()
        d_dbg_g = nc.dram_tensor("dbg_g", [128, L], BF16, kind="ExternalOutput").ap()
        d_dbg_s = nc.dram_tensor("dbg_s", [128, L], BF16, kind="ExternalOutput").ap()
        d_dbg_t1 = nc.dram_tensor("dbg_t1", [128, 8 * R], BF16, kind="ExternalOutput").ap()
        d_dbg_gsb = nc.dram_tensor("dbg_gsb", [128, 8192], BF16, kind="ExternalOutput").ap()
        d_dbg_psb = nc.dram_tensor("dbg_psb", [128, 4096], BF16, kind="ExternalOutput").ap()
        d_dbg_ysb = nc.dram_tensor("dbg_ysb", [128, 8192], BF16, kind="ExternalOutput").ap()
        d_dbg_vsb = nc.dram_tensor("dbg_vsb", [128, 1024], BF16, kind="ExternalOutput").ap()
        d_dbg_ib = nc.dram_tensor("dbg_ib", [128, 1024], F32, kind="ExternalOutput").ap()
        d_dbg_cs = nc.dram_tensor("dbg_cs", [128, 8], F32, kind="ExternalOutput").ap()
        d_dbg_ps = nc.dram_tensor("dbg_ps", [128, 512], F32, kind="ExternalOutput").ap()
    d_xdup = nc.dram_tensor("xdup", [R, 2 * L], BF16, kind="Internal").ap()
    d_gsp = nc.dram_tensor("gsp", [R, L], BF16, kind="Internal").ap()

    d_f1 = nc.dram_tensor("f1const", [64, 128], BF16, kind="ExternalInput").ap()
    d_f2st1 = nc.dram_tensor("f2st1", [128, 64 * 128], BF16, kind="ExternalInput").ap()
    d_f2st2 = nc.dram_tensor("f2st2", [128, 64 * 128], BF16, kind="ExternalInput").ap()
    d_e2048a = nc.dram_tensor("e2048a", [128, 4], BF16, kind="ExternalInput").ap()
    d_e2048b = nc.dram_tensor("e2048b", [128, 4], BF16, kind="ExternalInput").ap()
    d_fold2048 = nc.dram_tensor("fold2048", [4, 2], F32, kind="ExternalInput").ap()
    d_fold = nc.dram_tensor("fold", [128, 64], BF16, kind="ExternalInput").ap()
    d_winv = nc.dram_tensor("winv", [128, 128], BF16, kind="ExternalInput").ap()
    d_fcos = nc.dram_tensor("fcos", [128, 64 * 64], BF16, kind="ExternalInput").ap()
    d_fsin = nc.dram_tensor("fsin", [128, 64 * 64], BF16, kind="ExternalInput").ap()
    d_signmu = nc.dram_tensor("signmu", [128, 64], F32, kind="ExternalInput").ap()
    d_iotap = nc.dram_tensor("iotap", [128, 1], F32, kind="ExternalInput").ap()
    d_ident = nc.dram_tensor("ident", [128, 128], F32, kind="ExternalInput").ap()
    d_ones1 = nc.dram_tensor("ones1", [1, 128], F32, kind="ExternalInput").ap()
    d_onescol = nc.dram_tensor("onescol", [128, 1], BF16, kind="ExternalInput").ap()
    d_eps = nc.dram_tensor("epsc", [128, 1], F32, kind="ExternalInput").ap()

    sch = Sched()
    sch.__init_flush__()
    for s in ("dS", "dG", "tS", "vS", "sS", "gS", "oS"):
        sch.new_sem(s)

    import contextlib
    stack = contextlib.ExitStack()

    def sb(name, shape, dt):
        return stack.enter_context(nc.sbuf_tensor(name, shape, dt))

    with stack:
        c_f1 = sb("c_f1", [64, 128], BF16)
        c_f2st1 = sb("c_f2st1", [128, 64 * 128], BF16)
        c_f2st2 = sb("c_f2st2", [128, 64 * 128], BF16)
        c_e2048a = sb("c_e2048a", [128, 4], BF16)
        c_e2048b = sb("c_e2048b", [128, 4], BF16)
        c_fold2048 = sb("c_fold2048", [4, 2], F32)
        c_fold = sb("c_fold", [128, 64], BF16)
        c_winv = sb("c_winv", [128, 128], BF16)
        c_fcos = sb("c_fcos", [128, 64 * 64], BF16)
        c_fsin = sb("c_fsin", [128, 64 * 64], BF16)
        c_signmu = sb("c_signmu", [128, 64], F32)
        c_iotap = sb("c_iotap", [128, 1], F32)
        c_ident = sb("c_ident", [128, 128], F32)
        c_ones1 = sb("c_ones1", [1, 128], F32)
        c_onescol = sb("c_onescol", [128, 1], BF16)
        c_eps = sb("c_eps", [128, 1], F32)
        sv = sb("sv", [128, 8], F32)
        nsv = sb("nsv", [128, 8], F32)
        st_m = sb("st_m", [128, T], F32)
        st_istd = sb("st_istd", [128, T], F32)
        st_mistd = sb("st_mistd", [128, T], F32)
        st_i2 = sb("st_i2", [128, T], F32)
        st_tmp = sb("st_tmp", [128, T], F32)
        st_bias = sb("st_bias", [128, T], F32)
        colsum = sb("colsum", [128, 8], F32)
        vdum = sb("vdum", [1, 8], F32)
        gfence = sb("gfence", [128, 16], BF16)
        sdum = sb("sdum", [1, 8], F32)
        t1s = sb("t1s", [128, 8 * R], BF16)

        ps0 = stack.enter_context(nc.psum_tensor("ps0", [128, 512], F32))
        ps1 = stack.enter_context(nc.psum_tensor("ps1", [128, 512], F32))
        ps2 = stack.enter_context(nc.psum_tensor("ps2", [128, 1024], F32))
        ps3 = stack.enter_context(nc.psum_tensor("ps3", [128, 1024], F32))
        ps4 = stack.enter_context(nc.psum_tensor("ps4", [128, 512], F32))
        ps5 = stack.enter_context(nc.psum_tensor("ps5", [128, 512], F32))

        sem_objs = {}
        for s in ("dS", "dG", "tS", "vS", "sS", "gS", "oS"):
            sem_objs[s] = stack.enter_context(nc.semaphore(s))

        blk = stack.enter_context(nc.Block())

        def ld_sync(dst, src):
            return sch.op("sync", lambda e, d=dst, s=src: e.dma_start(out=d, in_=s), "dS", 16)

        sch.flush_fns = {"vector": lambda e: e.memset(vdum[:], 0.0),
                         "scalar": lambda e: e.memzero(sdum[:])}

        def flush_v():
            return sch.flush("vector")

        def flush_s():
            return sch.flush("scalar")

        for dst, src in [
            (c_f1[:], d_f1[:]), (c_f2st1[:], d_f2st1[:]), (c_f2st2[:], d_f2st2[:]),
            (c_e2048a[:], d_e2048a[:]), (c_e2048b[:], d_e2048b[:]),
            (c_fold2048[:], d_fold2048[:]), (c_fold[:], d_fold[:]),
            (c_winv[:], d_winv[:]), (c_fcos[:], d_fcos[:]), (c_fsin[:], d_fsin[:]),
            (c_signmu[:], d_signmu[:]), (c_iotap[:], d_iotap[:]),
            (c_ident[:], d_ident[:]), (c_ones1[:], d_ones1[:]),
            (c_onescol[:], d_onescol[:]),
            (c_eps[:], d_eps[:]),
            (sv[:], d_s[:]),
        ]:
            ld_sync(dst, src)
        consts_dS = sch.cnt("dS")
        sch.wait("vector", "dS", consts_dS)
        sch.op("vector", lambda e: e.tensor_scalar(out=nsv[:], in0=sv[:], scalar1=-1.0, scalar2=None, op0=ALU.mult), "vS")

        stL = contextlib.ExitStack()
        with stL:
            xrt = stL.enter_context(nc.sbuf_tensor("xrt", [128, L], F32))
            sqt = stL.enter_context(nc.sbuf_tensor("sqt", [128, L], F32))
            xbt = stL.enter_context(nc.sbuf_tensor("xbt", [128, L], BF16))
            for t in range(T):
                dS_x = ld_sync(xrt[:], d_xr[t * 128: t * 128 + 128, :])
                sch.wait("scalar", "dS", dS_x)
                sch.op("scalar", lambda e: e.activation(out=sqt[:], in_=xrt[:], func=AF.Square), "sS")
                sch.op("scalar", lambda e: e.copy(out=xbt[:], in_=xrt[:]), "sS")
                sS_sq = flush_s()
                sch.wait("vector", "sS", sS_sq)
                sch.op("vector", lambda e, t=t: e.reduce_sum(out=st_m[:, t:t+1], in_=xrt[:], axis=AX.X), "vS")
                vS_r = sch.op("vector", lambda e, t=t: e.reduce_sum(out=st_istd[:, t:t+1], in_=sqt[:], axis=AX.X), "vS")
                sch.wait("vector", "vS", vS_r)
                sch.op("vector", lambda e, t=t: e.tensor_scalar(out=st_m[:, t:t+1], in0=st_m[:, t:t+1], scalar1=1.0/L, scalar2=None, op0=ALU.mult), "vS")
                v0 = sch.op("vector", lambda e, t=t: e.tensor_scalar(out=st_istd[:, t:t+1], in0=st_istd[:, t:t+1], scalar1=1.0/L, scalar2=None, op0=ALU.mult), "vS")
                sch.wait("vector", "vS", v0)
                v1 = sch.op("vector", lambda e, t=t: e.tensor_tensor(out=st_tmp[:, t:t+1], in0=st_m[:, t:t+1], in1=st_m[:, t:t+1], op=ALU.mult), "vS")
                sch.wait("vector", "vS", v1)
                v2 = sch.op("vector", lambda e, t=t: e.tensor_tensor(out=st_istd[:, t:t+1], in0=st_istd[:, t:t+1], in1=st_tmp[:, t:t+1], op=ALU.subtract), "vS")
                sch.wait("scalar", "vS", v2)
                sch.op("scalar", lambda e, t=t: e.activation(out=st_istd[:, t:t+1], in_=st_istd[:, t:t+1], func=AF.Ln, bias=c_eps[:, 0:1]), "sS")
                sS_i = sch.op("scalar", lambda e, t=t: e.activation(out=st_istd[:, t:t+1], in_=st_istd[:, t:t+1], func=AF.Exp, scale=-0.5), "sS")
                sch.wait("vector", "sS", sS_i)
                sch.op("vector", lambda e, t=t: e.tensor_tensor(out=st_mistd[:, t:t+1], in0=st_m[:, t:t+1], in1=st_istd[:, t:t+1], op=ALU.mult), "vS")
                v3 = sch.op("vector", lambda e, t=t: e.tensor_tensor(out=st_i2[:, t:t+1], in0=st_istd[:, t:t+1], in1=st_istd[:, t:t+1], op=ALU.mult), "vS")
                sch.wait("vector", "vS", v3)
                v4 = sch.op("vector", lambda e, t=t: e.tensor_tensor(out=st_bias[:, t:t+1], in0=st_mistd[:, t:t+1], in1=st_mistd[:, t:t+1], op=ALU.mult), "vS")
                sch.wait("vector", "vS", v4)
                sch.op("vector", lambda e, t=t: e.tensor_scalar(out=st_bias[:, t:t+1], in0=st_bias[:, t:t+1], scalar1=-4096.0, scalar2=None, op0=ALU.mult), "vS")
                sch.wait("sync", "sS", sS_sq)
                sch.op("sync", lambda e, t=t: e.dma_start(out=d_xdup[t*128:(t+1)*128, 0:L], in_=xbt[:]), "dS", 16)
                sch.op("sync", lambda e, t=t: e.dma_start(out=d_xdup[t*128:(t+1)*128, L:2*L], in_=xbt[:]), "dS", 16)
                sch.wait("sync", "vS", vS_r)
        xdup_dS = sch.cnt("dS")
        stats_vS = sch.cnt("vS")

        stA = contextlib.ExitStack()
        with stA:
            vsb = stA.enter_context(nc.sbuf_tensor("vsb", [128, 32 * RANK], BF16))
            xtb = stA.enter_context(nc.sbuf_tensor("xtb", [128, 1024], BF16))
            istd_b = stA.enter_context(nc.sbuf_tensor("istd_b", [128, R], F32))
            mistd_b = stA.enter_context(nc.sbuf_tensor("mistd_b", [128, R], F32))
            rowv = stA.enter_context(nc.sbuf_tensor("rowv", [1, R], F32))
            rowv2 = stA.enter_context(nc.sbuf_tensor("rowv2", [1, R], F32))
            mm1u = stA.enter_context(nc.sbuf_tensor("mm1u", [128, 512], F32))
            mm1v = stA.enter_context(nc.sbuf_tensor("mm1v", [128, 512], F32))
            sch.wait("tensor", "vS", stats_vS)
            sch.wait("tensor", "dS", consts_dS)
            for t in range(T):
                sch.op("tensor", lambda e, t=t: e.matmul(ps2[0:1, t*128:(t+1)*128], st_istd[:, t:t+1], c_ident[:], start=True, stop=True), "tS")
                sch.op("tensor", lambda e, t=t: e.matmul(ps3[0:1, t*128:(t+1)*128], st_mistd[:, t:t+1], c_ident[:], start=True, stop=True), "tS")
            tS_rv = sch.cnt("tS")
            sch.wait("scalar", "tS", tS_rv)
            sch.op("scalar", lambda e: e.copy(out=rowv[:], in_=ps2[0:1, :]), "sS")
            sS_rv = sch.op("scalar", lambda e: e.copy(out=rowv2[:], in_=ps3[0:1, :]), "sS")
            sch.wait("tensor", "sS", sS_rv)
            for h in range(2):
                tS_b = sch.op("tensor", lambda e, h=h: e.matmul(ps0[:], c_ones1[:], rowv[0:1, h*512:(h+1)*512], start=True, stop=True), "tS")
                sch.wait("scalar", "tS", tS_b)
                sS_c = sch.op("scalar", lambda e, h=h: e.copy(out=istd_b[:, h*512:(h+1)*512], in_=ps0[:]), "sS")
                sch.wait("tensor", "sS", sS_c)
                tS_b2 = sch.op("tensor", lambda e, h=h: e.matmul(ps0[:], c_ones1[:], rowv2[0:1, h*512:(h+1)*512], start=True, stop=True), "tS")
                sch.wait("scalar", "tS", tS_b2)
                sS_c2 = sch.op("scalar", lambda e, h=h: e.copy(out=mistd_b[:, h*512:(h+1)*512], in_=ps0[:]), "sS")
                sch.wait("tensor", "sS", sS_c2)
            sch.wait("gpsimd", "vS", stats_vS)
            sch.wait("gpsimd", "dS", xdup_dS)
            sch.wait("gpsimd", "sS", sch.cnt("sS"))
            for k in range(32):
                sch.op("gpsimd", lambda e, k=k: e.dma_start(out=vsb[:, k*RANK:(k+1)*RANK], in_=d_v[k*128:(k+1)*128, :]), "dG", 16)
            dG_v = sch.cnt("dG")
            sch.wait("tensor", "dG", dG_v)
            for rs in range(8):
                for k in range(32):
                    sch.op("tensor", lambda e, k=k, rs=rs: e.matmul(
                        ps4[:, rs:rs+1], vsb[:, k*RANK+rs*128: k*RANK+(rs+1)*128],
                        c_onescol[:], start=(k == 0), stop=(k == 31)), "tS")
            tS_cs = sch.cnt("tS")
            sch.wait("scalar", "tS", tS_cs)
            sS_cs = sch.op("scalar", lambda e: e.copy(out=colsum[:], in_=ps4[:, 0:8]), "sS")
            sch.wait("tensor", "sS", sS_cs)

            psA = [ps0[:, :], ps1[:, :], ps2[:, 0:512], ps2[:, 512:1024],
                   ps3[:, 0:512], ps3[:, 512:1024], ps4[:, :], ps5[:, :]]
            if DEBUG:
                sch.wait("sync", "dG", sch.cnt("dG"))
                sch.op("sync", lambda e: e.dma_start(out=d_dbg_vsb[:], in_=vsb[:, 0:1024]), "oS", 16)
                sch.wait("sync", "sS", sch.cnt("sS"))
                sch.op("sync", lambda e: e.dma_start(out=d_dbg_ib[:], in_=istd_b[:]), "oS", 16)
                sch.op("sync", lambda e: e.dma_start(out=d_dbg_cs[:], in_=colsum[:]), "oS", 16)
            for half in range(2):
                sch.barrier()
                evac_done_vS = sch.cnt("vS")
                tS_chunk = {}
                for k in range(32):
                    if k >= 2:
                        sch.wait("gpsimd", "tS", tS_chunk[k - 2])
                    dG_x = sch.op("gpsimd", lambda e, k=k, half=half: e.dma_start(
                        out=xtb[:, (k % 2)*512:(k % 2)*512+512],
                        in_=d_xt[k*128:(k+1)*128, half*512:(half+1)*512]), "dG", 16)
                    sch.wait("tensor", "dG", dG_x)
                    if k == 0:
                        sch.wait("tensor", "vS", evac_done_vS)
                    for rs in range(8):
                        sch.op("tensor", lambda e, k=k, rs=rs: e.matmul(
                            psA[rs], vsb[:, k*RANK+rs*128: k*RANK+(rs+1)*128],
                            xtb[:, (k % 2)*512:(k % 2)*512+512],
                            start=(k == 0), stop=(k == 31)), "tS")
                    tS_chunk[k] = sch.cnt("tS")
                tS_mm1 = sch.cnt("tS")
                if DEBUG and half == 0:
                    sch.wait("scalar", "tS", tS_mm1)
                    sch.op("scalar", lambda e: e.copy(out=mm1u[:], in_=psA[0]), "sS")
                    sS_d = sch.cnt("sS")
                    sch.wait("tensor", "sS", sS_d)
                    sch.wait("sync", "sS", sS_d)
                    sch.op("sync", lambda e: e.dma_start(out=d_dbg_ps[:], in_=mm1u[:]), "oS", 16)
                    sch.wait("vector", "oS", sch.cnt("oS"))
                sch.wait("vector", "tS", tS_mm1)
                sch.wait("vector", "sS", sch.cnt("sS"))
                for rs in range(8):
                    v_a = sch.op("vector", lambda e, rs=rs, half=half: e.tensor_tensor(
                        out=mm1u[:], in0=psA[rs], in1=istd_b[:, half*512:(half+1)*512], op=ALU.mult), "vS")
                    sch.wait("vector", "vS", v_a)
                    v_b = sch.op("vector", lambda e, rs=rs, half=half: e.scalar_tensor_tensor(
                        out=mm1v[:], in0=mistd_b[:, half*512:(half+1)*512],
                        scalar=colsum[:, rs:rs+1], in1=mm1u[:], op0=ALU.mult, op1=ALU.subtract), "vS")
                    sch.wait("vector", "vS", v_b)
                    sch.op("vector", lambda e, rs=rs, half=half: e.tensor_scalar(
                        out=t1s[:, rs*R+half*512: rs*R+(half+1)*512], in0=mm1v[:],
                        scalar1=nsv[:, rs:rs+1], scalar2=None, op0=ALU.mult), "vS")

        flush_v()
        stB = contextlib.ExitStack()
        with stB:
            utb = stB.enter_context(nc.sbuf_tensor("utb", [128, 8 * L], BF16))
            gbuf = stB.enter_context(nc.sbuf_tensor("gbuf", [128, 2048], BF16))
            sch.wait("gpsimd", "tS", sch.cnt("tS"))
            sch.wait("gpsimd", "vS", sch.cnt("vS"))
            sch.wait("gpsimd", "sS", sch.cnt("sS"))
            sch.wait("gpsimd", "oS", sch.cnt("oS"))
            for rc in range(8):
                sch.op("gpsimd", lambda e, rc=rc: e.dma_start(out=utb[:, rc*L:(rc+1)*L], in_=d_ut[rc*128:(rc+1)*128, :]), "dG", 16)
            dG_ut = sch.cnt("dG")
            sch.wait("tensor", "dG", dG_ut)
            sch.wait("tensor", "vS", sch.cnt("vS"))
            psB = [ps2[:, 0:512], ps2[:, 512:1024], ps3[:, 0:512], ps3[:, 512:1024]]
            for t in range(T):
                for og in range(2):
                    for rc in range(8):
                        for oc in range(4):
                            sch.op("tensor", lambda e, t=t, og=og, rc=rc, oc=oc: e.matmul(
                                psB[oc], t1s[:, rc*R+t*128: rc*R+(t+1)*128],
                                utb[:, rc*L+(og*4+oc)*512: rc*L+(og*4+oc+1)*512],
                                start=(rc == 0), stop=(rc == 7)), "tS")
                    tS_g = sch.cnt("tS")
                    sch.wait("scalar", "tS", tS_g)
                    for oc in range(4):
                        sch.op("scalar", lambda e, oc=oc: e.activation(out=gbuf[:, oc*512:(oc+1)*512], in_=psB[oc], func=AF.Silu), "sS")
                    sS_g = flush_s()
                    sch.wait("sync", "sS", sS_g)
                    dS_g = sch.op("sync", lambda e, t=t, og=og: e.dma_start(
                        out=d_gsp[t*128:(t+1)*128, og*2048:(og+1)*2048], in_=gbuf[:]), "dS", 16)
                    sch.wait("scalar", "dS", dS_g)
                    sch.wait("tensor", "sS", sS_g)

        if DEBUG:
            sch.op("sync", lambda e: e.dma_start(out=d_dbg_st[:, 0*T:1*T], in_=st_m[:]), "oS", 16)
            sch.op("sync", lambda e: e.dma_start(out=d_dbg_st[:, 1*T:2*T], in_=st_istd[:]), "oS", 16)
            sch.op("sync", lambda e: e.dma_start(out=d_dbg_st[:, 2*T:3*T], in_=st_i2[:]), "oS", 16)
            sch.op("sync", lambda e: e.dma_start(out=d_dbg_st[:, 3*T:4*T], in_=st_bias[:]), "oS", 16)
            sch.op("sync", lambda e: e.dma_start(out=d_dbg_st[:, 4*T:5*T], in_=st_mistd[:]), "oS", 16)
            sch.op("sync", lambda e: e.dma_start(out=d_dbg_t1[:], in_=t1s[:]), "oS", 16)

        stC = contextlib.ExitStack()
        with stC:
            def sbc(name, shape, dt):
                return stC.enter_context(nc.sbuf_tensor(name, shape, dt))
            xg = sbc("xg", [64, 128 * 64], BF16)
            big1 = sbc("big1", [128, 8192], BF16)
            psb_ = sbc("psb_", [128, 4096], BF16)
            sq64 = sbc("sq64", [128, 1024], BF16)
            scr16 = sbc("scr16", [128, 4096], F32)
            x2a = sbc("x2a", [4, 64], F32)
            x2b = sbc("x2b", [4, 64], F32)
            p2row = sbc("p2row", [2, 64], F32)
            p2048 = sbc("p2048", [128, 1], F32)
            p2e = sbc("p2e", [128, 1], F32)
            corr = sbc("corr", [128, 4096], F32)
            vals8 = sbc("vals8", [128, 8], F32)
            idx8 = sbc("idx8", [128, 8], U32)
            idxf = sbc("idxf", [128, 8], F32)
            w8 = sbc("w8", [128, 8], F32)
            w8n = sbc("w8n", [128, 8], F32)
            wsc = sbc("wsc", [128, 4], F32)
            sas = sbc("sas", [128, 2], F32)
            idxu = sbc("idxu", [128, 8], U32)
            sacc = sbc("sacc", [128, 4096], BF16)
            gout = sbc("gout", [128, 4096], BF16)
            xrt2 = sbc("xrt2", [128, 4096], F32)

            gsb = big1
            ysb = big1
            rollsv = scr16[:].bitcast(BF16)

            def init_breg(e):
                e._breg = e.alloc_register("breg")
                return e.reg_mov(e._breg, BOUND)
            sch.op("gpsimd", init_breg, "gS")

            prev = None
            for t in range(T):
                if prev is None:
                    sch.barrier()
                else:
                    sch.wait("tensor", "sS", prev["sS"])
                    sch.wait("tensor", "dG", prev["dG"])
                    sch.wait("scalar", "tS", prev["tS"])
                    sch.wait("scalar", "vS", prev["vS_ix"])
                    sch.wait("scalar", "dG", prev["dG"])
                    sch.wait("vector", "oS", prev["oS"])
                    sch.wait("vector", "sS", prev["sS"])
                    sch.wait("vector", "dG", prev["dG"])
                    sch.wait("vector", "tS", prev["tS"])
                    sch.wait("gpsimd", "tS", prev["tS"])
                    sch.wait("gpsimd", "oS", prev["oS"])
                    sch.wait("gpsimd", "vS", prev["vS"])
                    sch.wait("sync", "vS", prev["vS"])
                    sch.wait("sync", "oS", prev["oS"])
                dG_xg = sch.op("gpsimd", lambda e, t=t: e.dma_start(out=xg[:], in_=d_xf3[:, t*8192:(t+1)*8192]), "dG", 16)
                dS_x2 = ld_sync(xrt2[:], d_xr[t*128:(t+1)*128, :])
                dS_g2 = ld_sync(gout[:], d_gsp[t*128:(t+1)*128, :])

                sch.wait("tensor", "dG", dG_xg)


                f1_psums = [ps0, ps1]
                f1_evac = {}
                for grp in range(16):
                    pb = f1_psums[grp % 2]
                    if grp >= 2:
                        sch.wait("tensor", "sS", f1_evac[grp - 2])
                    for pi in range(4):
                        pair = grp * 4 + pi
                        sch.op("tensor", lambda e, pair=pair, pb=pb, pi=pi: e.matmul(
                            pb[:, pi*128:(pi+1)*128],
                            xg[:, pair*128:(pair+1)*128],
                            c_f1[:], start=True, stop=True), "tS")
                    tS_f1 = sch.cnt("tS")
                    sch.wait("scalar", "tS", tS_f1)
                    sch.op("scalar", lambda e, grp=grp, pb=pb: e.copy(
                        out=gsb[:, grp*512:(grp+1)*512], in_=pb[:]), "sS")
                    f1_evac[grp] = flush_s()
                sS_g_all = flush_s()
                sch.wait("tensor", "sS", sS_g_all)
                if DEBUG and t == 0:
                    sch.wait("sync", "sS", sS_g_all)
                    sch.op("sync", lambda e: e.dma_start(out=d_dbg_gsb[:], in_=big1[:]), "oS", 16)

                f2_psums = [ps2, ps3]
                sq_done = {}
                pevac_done = {}
                for ph in range(4):
                    pb = f2_psums[ph % 2]
                    if ph >= 2:
                        sch.wait("tensor", "sS", sq_done[ph - 2])
                    for din in range(16):
                        d = ph * 16 + din
                        sch.op("tensor", lambda e, d=d, din=din, pb=pb: e.matmul(
                            pb[:, din*64:(din+1)*64], c_f2st1[:, d*128:(d+1)*128],
                            gsb[:].rearrange("p (q f) -> p f q", q=64)[:, 0*64 + d, :],
                            start=True, stop=False), "tS")
                        sch.op("tensor", lambda e, d=d, din=din, pb=pb: e.matmul(
                            pb[:, din*64:(din+1)*64], c_f2st2[:, d*128:(d+1)*128],
                            gsb[:].rearrange("p (q f) -> p f q", q=64)[:, 64 + d, :],
                            start=False, stop=True), "tS")
                    if ph == 0:
                        sch.op("tensor", lambda e: e.matmul(ps4[0:4, 0:64], c_e2048a[:], gsb[:].rearrange("p (q f) -> p f q", q=64)[:, 0, :], start=True, stop=False), "tS")
                        sch.op("tensor", lambda e: e.matmul(ps4[0:4, 0:64], c_e2048b[:], gsb[:].rearrange("p (q f) -> p f q", q=64)[:, 64, :], start=False, stop=True), "tS")
                    tS_f2 = sch.cnt("tS")
                    sch.wait("scalar", "tS", tS_f2)
                    sch.op("scalar", lambda e, pb=pb: e.activation(out=sq64[0:64, :], in_=pb[0:64, :], func=AF.Square), "sS")
                    sch.op("scalar", lambda e, pb=pb: e.activation(out=sq64[64:128, :], in_=pb[64:128, :], func=AF.Square), "sS")
                    sq_done[ph] = flush_s()
                    sch.wait("tensor", "sS", sq_done[ph])
                    for nh in range(2):
                        tS_fo = sch.op("tensor", lambda e, nh=nh: e.matmul(
                            ps5[0:64, :], c_fold[:], sq64[:, nh*512:(nh+1)*512],
                            start=True, stop=True), "tS")
                        sch.wait("scalar", "tS", tS_fo)
                        sch.op("scalar", lambda e, ph=ph, nh=nh: e.copy(
                            out=psb_[0:32, ph*1024 + nh*512: ph*1024 + (nh+1)*512],
                            in_=ps5[0:32, :]), "sS")
                        sch.op("scalar", lambda e, ph=ph, nh=nh: e.copy(
                            out=psb_[64:96, ph*1024 + nh*512: ph*1024 + (nh+1)*512],
                            in_=ps5[32:64, :]), "sS")
                        sS_pe = flush_s()
                        pevac_done[(ph, nh)] = sS_pe
                        sch.wait("tensor", "sS", sS_pe)
                sS_pall = sch.cnt("sS")
                sch.wait("vector", "sS", sS_pall)
                sch.op("vector", lambda e: e.tensor_scalar(out=psb_[0:1, 0:64], in0=psb_[0:1, 0:64], scalar1=0.5, scalar2=None, op0=ALU.mult), "vS")
                sch.op("vector", lambda e: e.tensor_scalar(out=psb_[64:65, 0:64], in0=psb_[64:65, 0:64], scalar1=0.5, scalar2=None, op0=ALU.mult), "vS")
                vS_ph = flush_v()

                if DEBUG and t == 0:
                    sch.wait("sync", "vS", vS_ph)
                    sch.op("sync", lambda e: e.dma_start(out=d_dbg_psb[:], in_=psb_[:]), "oS", 16)

                tS_e = sch.cnt("tS")
                sch.wait("scalar", "tS", tS_e)
                sS_x2 = sch.op("scalar", lambda e: e.copy(out=x2a[:], in_=ps4[0:4, 0:64]), "sS")
                sch.wait("scalar", "sS", sS_x2)
                sS_x2b = sch.op("scalar", lambda e: e.activation(out=x2b[:], in_=x2a[:], func=AF.Square), "sS")
                sch.wait("tensor", "sS", sS_x2b)
                tS_p2 = sch.op("tensor", lambda e: e.matmul(ps4[0:2, 64:128], c_fold2048[:], x2b[:], start=True, stop=True), "tS")
                sch.wait("scalar", "tS", tS_p2)
                sS_p2r = sch.op("scalar", lambda e: e.copy(out=p2row[:], in_=ps4[0:2, 64:128]), "sS")
                sch.wait("tensor", "sS", sS_p2r)
                tS_p2t = sch.op("tensor", lambda e: e.matmul(ps4[0:64, 128:130], p2row[:], c_ident[0:2, 0:2], start=True, stop=True), "tS")
                sch.wait("scalar", "tS", tS_p2t)
                sch.op("scalar", lambda e: e.copy(out=p2048[0:64, :], in_=ps4[0:64, 128:129]), "sS")
                sS_p2c = sch.op("scalar", lambda e: e.copy(out=p2048[64:128, :], in_=ps4[0:64, 129:130]), "sS")
                sch.wait("vector", "sS", sS_p2c)
                v_p = sch.op("vector", lambda e, t=t: e.tensor_tensor(out=p2e[:], in0=p2048[:], in1=st_i2[:, t:t+1], op=ALU.mult), "vS")
                sch.wait("vector", "vS", v_p)
                vS_p2e = sch.op("vector", lambda e: e.tensor_scalar(out=p2e[:], in0=p2e[:], scalar1=1.0/4096.0, scalar2=None, op0=ALU.mult), "vS")

                sch.wait("tensor", "vS", vS_ph)
                if DEBUG and t == 0:
                    sch.wait("scalar", "oS", sch.cnt("oS"))
                psb3 = psb_[:].rearrange("p (d q) -> p d q", d=64)
                i1_psums = [ps2, ps3]
                i1_evac = {}
                for g in range(8):
                    pb = i1_psums[g % 2]
                    if g >= 2:
                        sch.wait("tensor", "sS", i1_evac[g - 2])
                    for slot in range(8):
                        pr = g * 8 + slot
                        for r2 in range(2):
                            sch.op("tensor", lambda e, pr=pr, slot=slot, r2=r2, pb=pb: e.matmul(
                                pb[64*r2:64*r2+64, slot*128:(slot+1)*128],
                                psb3[64*r2:64*r2+32, :, pr],
                                c_winv[64*r2:64*r2+32, :],
                                start=True, stop=True), "tS")
                    tS_i1 = sch.cnt("tS")
                    sch.wait("scalar", "tS", tS_i1)
                    sch.op("scalar", lambda e, g=g, pb=pb: e.copy(
                        out=ysb[:, g*1024:(g+1)*1024], in_=pb[:]), "sS")
                    i1_evac[g] = flush_s()
                sch.wait("tensor", "sS", sch.cnt("sS"))

                if DEBUG and t == 0:
                    sch.wait("sync", "sS", sch.cnt("sS"))
                    sch.op("sync", lambda e: e.dma_start(out=d_dbg_ysb[:], in_=big1[:]), "oS", 16)


                i2_psums = [ps2, ps3]
                i2_evac = {}
                for ph in range(4):
                    pb = i2_psums[ph % 2]
                    if ph >= 2:
                        sch.wait("tensor", "sS", i2_evac[ph - 2])
                    for min_ in range(16):
                        mu = ph * 16 + min_
                        for r2 in range(2):
                            sch.op("tensor", lambda e, mu=mu, min_=min_, r2=r2, pb=pb: e.matmul(
                                pb[64*r2:64*r2+64, min_*64:(min_+1)*64],
                                ysb[64*r2:64*r2+64, :].rearrange("p (r f) -> p f r", r=64)[:, 0*64 + mu, :],
                                c_fcos[64*r2:64*r2+64, mu*64:(mu+1)*64],
                                start=True, stop=False), "tS")
                            sch.op("tensor", lambda e, mu=mu, min_=min_, r2=r2, pb=pb: e.matmul(
                                pb[64*r2:64*r2+64, min_*64:(min_+1)*64],
                                ysb[64*r2:64*r2+64, :].rearrange("p (r f) -> p f r", r=64)[:, 64 + mu, :],
                                c_fsin[64*r2:64*r2+64, mu*64:(mu+1)*64],
                                start=False, stop=True), "tS")
                    tS_i2 = sch.cnt("tS")
                    sch.wait("scalar", "tS", tS_i2)
                    sch.op("scalar", lambda e, ph=ph, pb=pb, t=t: e.activation(
                        out=corr[:].rearrange("p (lam mu) -> p mu lam", mu=64)[:, ph*16:(ph+1)*16, :],
                        in_=pb[:].rearrange("p (mi lam) -> p mi lam", mi=16),
                        func=AF.Identity, bias=st_bias[:, t:t+1], scale=st_i2[:, t:t+1]), "sS")
                    i2_evac[ph] = flush_s()
                sS_corr = sch.cnt("sS")

                sch.wait("vector", "sS", sS_corr)
                sch.wait("vector", "vS", vS_p2e)
                corr3 = corr[:].rearrange("p (lam mu) -> p lam mu", mu=64)
                sgb = c_signmu[:].unsqueeze(1).broadcast_to((128, 64, 64))
                vS_ed = sch.op("vector", lambda e: e.scalar_tensor_tensor(
                    out=corr3, in0=sgb, scalar=p2e[:, 0:1], in1=corr3,
                    op0=ALU.mult, op1=ALU.add), "vS")

                sch.wait("vector", "vS", vS_ed)
                vS_mx = sch.op("vector", lambda e: e.max(out=vals8[:], in_=corr[:]), "vS")
                sch.wait("vector", "vS", vS_mx)
                vS_ix = sch.op("vector", lambda e: e.max_index(out=idx8[:], in_max=vals8[:], in_values=corr[:]), "vS")
                sch.wait("vector", "vS", vS_ix)
                vS_v0 = sch.op("vector", lambda e: e.tensor_scalar(out=wsc[:, 0:1], in0=vals8[:, 0:1], scalar1=-1.0, scalar2=None, op0=ALU.mult), "vS")
                sch.wait("scalar", "vS", vS_v0)
                sS_exp = sch.op("scalar", lambda e: e.activation(out=w8[:], in_=vals8[:], func=AF.Exp, bias=wsc[:, 0:1]), "sS")
                sch.wait("vector", "sS", sS_exp)
                vS_sw = sch.op("vector", lambda e: e.reduce_sum(out=wsc[:, 1:2], in_=w8[:], axis=AX.X), "vS")
                sch.wait("vector", "vS", vS_sw)
                vS_rw = sch.op("vector", lambda e: e.reciprocal(out=wsc[:, 2:3], in_=wsc[:, 1:2]), "vS")
                sch.wait("vector", "vS", vS_rw)
                vS_wn = sch.op("vector", lambda e: e.tensor_scalar(out=w8n[:], in0=w8[:], scalar1=wsc[:, 2:3], scalar2=None, op0=ALU.mult), "vS")
                sch.wait("vector", "vS", vS_wn)
                vS_sw2 = sch.op("vector", lambda e: e.reduce_sum(out=wsc[:, 3:4], in_=w8n[:], axis=AX.X), "vS")
                sch.op("vector", lambda e: e.tensor_copy(out=idxf[:], in_=idx8[:]), "vS")
                vS_if = sch.op("vector", lambda e, t=t: e.tensor_scalar(
                    out=idxf[:], in0=idxf[:], scalar1=c_iotap[:, 0:1], scalar2=float(t*128*8192), op0=ALU.add, op1=ALU.add), "vS")
                sch.wait("vector", "vS", vS_if)
                vS_msk = sch.op("vector", lambda e: e.tensor_scalar(
                    out=w8[:], in0=w8n[:], scalar1=0.0, scalar2=OOB_OFF, op0=ALU.is_equal, op1=ALU.mult), "vS")
                sch.wait("vector", "vS", vS_msk)
                vS_ia = sch.op("vector", lambda e: e.tensor_tensor(out=idxf[:], in0=idxf[:], in1=w8[:], op=ALU.add), "vS")
                sch.wait("vector", "vS", vS_ia)
                sch.op("vector", lambda e: e.tensor_copy(out=idxu[:], in_=idxf[:]), "vS")
                vS_iu = flush_v()
                sch.wait("vector", "vS", vS_sw2)
                v_s1 = sch.op("vector", lambda e, t=t: e.tensor_tensor(out=sas[:, 1:2], in0=st_mistd[:, t:t+1], in1=wsc[:, 3:4], op=ALU.mult), "vS")
                sch.wait("vector", "vS", v_s1)
                vS_sa = sch.op("vector", lambda e: e.tensor_scalar(out=sas[:, 1:2], in0=sas[:, 1:2], scalar1=-1.0, scalar2=None, op0=ALU.mult), "vS")

                sch.op("vector", lambda e: e.memset(rollsv[:, 0:4096], 0.0), "vS")
                sch.op("vector", lambda e: e.memset(rollsv[:, 4096:8192], 0.0), "vS")
                vS_mz = flush_v()
                sch.wait("gpsimd", "vS", vS_iu)
                sch.wait("gpsimd", "vS", vS_mz)
                sch.wait("gpsimd", "dS", xdup_dS)
                s_steps = {}
                for kk in range(8):
                    rb = rollsv[:, (kk % 2)*4096:(kk % 2)*4096+4096]
                    if kk >= 2:
                        sch.wait("gpsimd", "vS", s_steps[kk - 2])
                    sch.op("gpsimd", lambda e, kk=kk, rb=rb: e.indirect_dma_start(
                        rb, None, d_xdup[:], IndirectOffsetOnAxis(idxu[:, kk:kk+1], 1),
                        bounds_check=e._breg, oob_is_err=False), "dG", 16)
                    dG_gk = sch.op("gpsimd", lambda e, kk=kk: e.indirect_dma_start(
                        psb_[:], None, d_xdup[:], IndirectOffsetOnAxis(idxu[:, kk:kk+1], 1),
                        bounds_check=e._breg, oob_is_err=False), "dG", 16)
                    sch.wait("vector", "dG", dG_gk)
                    if kk == 0:
                        s_steps[0] = sch.op("vector", lambda e, rb=rb: e.tensor_scalar(
                            out=sacc[:], in0=rb, scalar1=w8n[:, 0:1], scalar2=None, op0=ALU.mult), "vS")
                    else:
                        sch.wait("vector", "vS", s_steps[kk - 1])
                        s_steps[kk] = sch.op("vector", lambda e, kk=kk, rb=rb: e.scalar_tensor_tensor(
                            out=sacc[:], in0=rb, scalar=w8n[:, kk:kk+1], in1=sacc[:], op0=ALU.mult, op1=ALU.add), "vS")
                vS_s = s_steps[7]

                sch.wait("vector", "vS", vS_s)
                sch.wait("vector", "vS", vS_sa)
                vS_u = sch.op("vector", lambda e, t=t: e.tensor_scalar(
                    out=sacc[:], in0=sacc[:], scalar1=st_istd[:, t:t+1], scalar2=sas[:, 1:2], op0=ALU.mult, op1=ALU.add), "vS")
                sch.wait("vector", "dS", dS_g2)
                sch.wait("vector", "vS", vS_u)
                vS_vv = sch.op("vector", lambda e: e.tensor_tensor(out=scr16[:], in0=sacc[:], in1=gout[:], op=ALU.mult), "vS")
                sch.wait("vector", "dS", dS_x2)
                sch.wait("vector", "vS", vS_vv)
                vS_o = sch.op("vector", lambda e: e.tensor_tensor(out=scr16[:], in0=scr16[:], in1=xrt2[:], op=ALU.add), "vS")
                if DEBUG and t == 0:
                    sch.wait("sync", "vS", vS_ed)
                    sch.op("sync", lambda e: e.dma_start(out=d_dbg_corr[:], in_=corr[:]), "oS", 16)
                    sch.wait("sync", "vS", vS_ix)
                    sch.op("sync", lambda e: e.dma_start(out=d_dbg_v8[:], in_=vals8[:]), "oS", 16)
                    sch.op("sync", lambda e: e.dma_start(out=d_dbg_i8[:], in_=idx8[:]), "oS", 16)
                    sch.wait("sync", "vS", vS_wn)
                    sch.op("sync", lambda e: e.dma_start(out=d_dbg_w8[:], in_=w8n[:]), "oS", 16)
                    sch.wait("sync", "dS", dS_g2)
                    sch.op("sync", lambda e: e.dma_start(out=d_dbg_g[:], in_=gout[:]), "oS", 16)
                    sch.wait("sync", "vS", vS_u)
                    sch.op("sync", lambda e: e.dma_start(out=d_dbg_s[:], in_=sacc[:]), "oS", 16)
                sch.wait("sync", "vS", vS_o)
                sch.op("sync", lambda e, t=t: e.dma_start(out=d_out[t*128:(t+1)*128, :], in_=scr16[:]), "oS", 16)
                prev = {"sS": sch.cnt("sS"), "tS": sch.cnt("tS"), "vS": sch.cnt("vS"),
                        "dG": sch.cnt("dG"), "oS": sch.cnt("oS"), "vS_ix": vS_ix}

            sch.wait("sync", "oS", sch.cnt("oS"))

        @blk.sync
        def _(sync):
            sch.emit("sync", sync, sem_objs)

        @blk.vector
        def _(vector):
            sch.emit("vector", vector, sem_objs)

        @blk.scalar
        def _(scalar):
            sch.emit("scalar", scalar, sem_objs)

        @blk.tensor
        def _(tensor):
            sch.emit("tensor", tensor, sem_objs)

        @blk.gpsimd
        def _(gpsimd):
            sch.emit("gpsimd", gpsimd, sem_objs)

    return nc


def kernel(L0_attn_x, L0_ffn_U, L0_ffn_S, L0_ffn_V):
    x = np.ascontiguousarray(np.asarray(L0_attn_x, dtype=np.float32).reshape(8192, L))
    U = np.asarray(L0_ffn_U, dtype=np.float32)
    S = np.asarray(L0_ffn_S, dtype=np.float32)
    V = np.asarray(L0_ffn_V, dtype=np.float32)
    UT = np.ascontiguousarray(U.T)
    c = build_consts()
    nc = build_program()
    in_maps = []
    for ci in range(N_CORES):
        rows = slice(ci * R, (ci + 1) * R)
        xr = x[rows]
        xf3 = np.ascontiguousarray(xr.reshape(T, 2, 64, 64, 64).transpose(3, 0, 2, 1, 4)).reshape(64, R * 64)
        xt = np.ascontiguousarray(xr.T)
        in_maps.append({
            "xr": xr, "xf3": xf3, "xt": xt, "vw": V, "ut": UT,
            "sw": np.ascontiguousarray(S.reshape(8, 128).T),
            "f1const": c["f1const"], "f2st1": c["f2st1"], "f2st2": c["f2st2"],
            "e2048a": c["e2048a"], "e2048b": c["e2048b"], "fold2048": c["fold2048"],
            "fold": c["fold"], "winv": c["winv"], "fcos": c["fcos"], "fsin": c["fsin"],
            "signmu": c["signmu"], "iotap": c["iotap"], "ident": c["ident"],
            "ones1": c["ones1"], "onescol": c["onescol"], "epsc": c["epsc"],
        })
    res = run_bass_kernel_spmd(nc, in_maps, core_ids=list(range(N_CORES)))
    out = np.concatenate([r["out"] for r in res.results], axis=0)
    return out.reshape(2, 4096, 4096)
